# version 04
# baseline (speedup 1.0000x reference)
import numpy as np
import ml_dtypes
import concourse.bass as bass
import concourse.mybir as mybir
from concourse.bass_utils import run_bass_kernel_spmd

F32 = mybir.dt.float32
BF16 = mybir.dt.bfloat16
I32 = mybir.dt.int32
I8 = mybir.dt.int8
AF = mybir.ActivationFunctionType
ALU = mybir.AluOpType
AXX = mybir.AxisListType.X

B, S, DM = 8, 2048, 4096
NH, NKV, HD = 16, 4, 64
LAT, KVD = NH * HD, NKV * HD
DFF, R = 11008, 16
DFFP = 11264
NT = S // 128
EPS = 1e-5

NF4 = np.array([-1.0, -0.6961928009986877, -0.5250730514526367, -0.39491748809814453,
                -0.28444138169288635, -0.18477343022823334, -0.09105003625154495, 0.0,
                0.07958029955625534, 0.16093020141124725, 0.2461123019456863,
                0.33791524171829224, 0.44070982933044434, 0.5626170039176941,
                0.7229568362236023, 1.0], dtype=np.float64)

PAIRS = [(0, 4), (1, 5), (2, 6), (3, 7), (8, 12), (9, 13), (10, 14), (11, 15)]

D_SHARD_ROWS = [1408] * 6 + [1280, 1280]
D_PAD = 1408
D_START = [0, 1408, 2816, 4224, 5632, 7040, 8448, 9728]


def _prep(inp):
    c = np.ascontiguousarray
    f32 = np.float32
    wqkvT = c(np.asarray(inp["L0_attn_w_qkv"]).T.astype(f32, copy=False))
    woT = np.asarray(inp["L0_attn_w_o"]).T
    perm = []
    for (a, b) in PAIRS:
        perm.extend(range(a * 64, a * 64 + 64))
        perm.extend(range(b * 64, b * 64 + 64))
    woTp = c(woT[np.array(perm), :].astype(f32, copy=False))
    qc1T = c(np.asarray(inp["L0_attn_qc1_w"]).reshape(16, 64, 64, 3).transpose(3, 0, 2, 1).astype(f32, copy=False))
    qc2T = c(np.asarray(inp["L0_attn_qc2_w"]).transpose(2, 1, 0).astype(f32, copy=False))
    kc1T = c(np.asarray(inp["L0_attn_kc1_w"]).reshape(4, 64, 64, 3).transpose(3, 0, 2, 1).astype(f32, copy=False))
    kc2T = c(np.asarray(inp["L0_attn_kc2_w"]).transpose(2, 1, 0).astype(f32, copy=False))
    qc1b = c(np.asarray(inp["L0_attn_qc1_b"]).reshape(8, 128).T.astype(f32, copy=False))
    qc2b = c(np.asarray(inp["L0_attn_qc2_b"]).reshape(8, 128).T.astype(f32, copy=False))
    kc1b = c(np.asarray(inp["L0_attn_kc1_b"]).reshape(2, 128).T.astype(f32, copy=False))
    kc2b = c(np.asarray(inp["L0_attn_kc2_b"]).reshape(2, 128).T.astype(f32, copy=False))
    ktemp = c(np.asarray(inp["L0_attn_key_temp"]).reshape(1, 1).astype(f32, copy=False))
    gT = np.asarray(inp["L0_ffn_g_codes"]).T
    gaT = np.asarray(inp["L0_ffn_g_absmax"]).T
    uT = np.asarray(inp["L0_ffn_u_codes"]).T
    uaT = np.asarray(inp["L0_ffn_u_absmax"]).T
    dT = np.asarray(inp["L0_ffn_d_codes"]).T
    daT = np.asarray(inp["L0_ffn_d_absmax"]).T
    x_all = np.asarray(inp["L0_attn_x"])

    def padc(a):
        o = np.zeros((a.shape[0], DFFP), np.int32)
        o[:, :DFF] = a
        return o

    def padf(a):
        o = np.zeros((a.shape[0], DFFP), f32)
        o[:, :DFF] = a.astype(f32, copy=False)
        return o

    glb_p = np.zeros((R, DFFP), f32); glb_p[:, :DFF] = np.asarray(inp["L0_ffn_g_lora_b"])
    ulb_p = np.zeros((R, DFFP), f32); ulb_p[:, :DFF] = np.asarray(inp["L0_ffn_u_lora_b"])

    maps = []
    for r in range(8):
        f0, nrow = D_START[r], D_SHARD_ROWS[r]
        dc = np.zeros((D_PAD, DM), np.int32)
        dc[:nrow] = dT[f0:f0 + nrow]
        da = np.zeros((D_PAD // 64, DM), f32)
        da[:nrow // 64] = daT[f0 // 64:(f0 + nrow) // 64]
        maps.append({
            "x": c(x_all[r].astype(f32, copy=False)),
            "wqkvT": wqkvT, "woTp": woTp,
            "qc1T": qc1T, "qc2T": qc2T, "kc1T": kc1T, "kc2T": kc2T,
            "qc1b": qc1b, "qc2b": qc2b, "kc1b": kc1b, "kc2b": kc2b,
            "ktemp": ktemp,
            "gla": c(np.asarray(inp["L0_ffn_g_lora_a"]).astype(f32, copy=False)),
            "glb": glb_p,
            "ula": c(np.asarray(inp["L0_ffn_u_lora_a"]).astype(f32, copy=False)),
            "ulb": ulb_p,
            "dla": c(np.asarray(inp["L0_ffn_d_lora_a"]).astype(f32, copy=False)),
            "dlb": c(np.asarray(inp["L0_ffn_d_lora_b"]).astype(f32, copy=False)),
            "gcodes": padc(gT[512 * r:512 * r + 512]),
            "gabs": padf(gaT[8 * r:8 * r + 8]),
            "ucodes": padc(uT[512 * r:512 * r + 512]),
            "uabs": padf(uaT[8 * r:8 * r + 8]),
            "dcodes": dc, "dabs": da,
        })
    return maps


class Ctr:
    def __init__(self, nc, name):
        self._cm = nc.semaphore(name)
        self.h = self._cm.__enter__()
        self.n = 0

    def inc(self, inst, k=1):
        inst.then_inc(self.h, k)
        self.n += k
        return self.n


def fence(eng, ctr, inst):
    ctr.inc(inst)
    eng.wait_ge(ctr.h, ctr.n)


class Env:
    def __init__(self, nc, block):
        self.nc = nc
        self.block = block
        self.sy = Ctr(nc, "c_sy")
        self.gp = Ctr(nc, "c_gp")
        self.sc = Ctr(nc, "c_sc")
        self.ve = Ctr(nc, "c_ve")
        self.ac = Ctr(nc, "c_ac")
        self.pe = Ctr(nc, "c_pe")
        self.cc = Ctr(nc, "c_cc")
        self.st = Ctr(nc, "c_st")
        self.mark = {}


def _ln_core(E, xbuf, scr, stat, W):
    @E.block.vector
    def _(v):
        fence(v, E.ve, v.tensor_reduce(stat[:, 0:1], xbuf[:, 0:W], AXX, ALU.add))
        fence(v, E.ve, v.tensor_scalar(stat[:, 1:2], stat[:, 0:1], -1.0 / W, None, ALU.mult))
        fence(v, E.ve, v.tensor_scalar(xbuf[:, 0:W], xbuf[:, 0:W], stat[:, 1:2], None, ALU.add))

    @E.block.scalar
    def _(s):
        s.wait_ge(E.ve.h, E.ve.n)
        fence(s, E.ac, s.activation(scr[:, 0:W], xbuf[:, 0:W], AF.Square, accum_out=stat[:, 2:3]))

    @E.block.vector
    def _(v):
        v.wait_ge(E.ac.h, E.ac.n)
        fence(v, E.ve, v.tensor_scalar(stat[:, 2:3], stat[:, 2:3], 1.0 / W, EPS, ALU.mult, ALU.add))

    @E.block.scalar
    def _(s):
        s.wait_ge(E.ve.h, E.ve.n)
        fence(s, E.ac, s.activation(stat[:, 2:3], stat[:, 2:3], AF.Ln))
        fence(s, E.ac, s.activation(stat[:, 3:4], stat[:, 2:3], AF.Exp, scale=-0.5))


def build(dbg=(), attn=True, ffn=True):
    dbg = set(dbg)
    nc = bass.Bass("TRN2", target_bir_lowering=False, debug=False)
    dt, st = nc.dram_tensor, nc.sbuf_tensor

    T = {}
    T["x_in"] = dt("x", [S, DM], F32, kind="ExternalInput").ap()
    T["out_t"] = dt("out", [S, DM], F32, kind="ExternalOutput").ap()
    for nm, shape in [("wqkvT", [DM, 1536]), ("woTp", [LAT, DM]),
                      ("qc1T", [3, 16, 64, 64]), ("qc2T", [3, 1024, 1024]),
                      ("kc1T", [3, 4, 64, 64]), ("kc2T", [3, 256, 256]),
                      ("qc1b", [128, 8]), ("qc2b", [128, 8]),
                      ("kc1b", [128, 2]), ("kc2b", [128, 2]), ("ktemp", [1, 1]),
                      ("gla", [DM, R]), ("glb", [R, DFFP]), ("ula", [DM, R]),
                      ("ulb", [R, DFFP]), ("dla", [DFF, R]), ("dlb", [R, DM]),
                      ("gabs", [8, DFFP]), ("uabs", [8, DFFP]), ("dabs", [D_PAD // 64, DM])]:
        T[nm] = dt(nm, shape, F32, kind="ExternalInput").ap()
    for nm, shape in [("gcodes", [512, DFFP]), ("ucodes", [512, DFFP]), ("dcodes", [D_PAD, DM])]:
        T[nm] = dt(nm, shape, I32, kind="ExternalInput").ap()

    T["wg_f"] = dt("wg_f", [DM, DFFP], BF16, kind="Internal").ap()
    T["wu_f"] = dt("wu_f", [DM, DFFP], BF16, kind="Internal").ap()
    T["wd_f"] = dt("wd_f", [8 * D_PAD, DM], BF16, kind="Internal").ap()
    T["wg_sh"] = dt("wg_sh", [512, DFFP], BF16, kind="Internal").ap()
    T["wu_sh"] = dt("wu_sh", [512, DFFP], BF16, kind="Internal").ap()
    T["wd_sh"] = dt("wd_sh", [D_PAD, DM], BF16, kind="Internal").ap()
    T["x1_d"] = dt("x1_d", [S, DM], F32, kind="Internal" if attn else "ExternalInput").ap()
    T["xn2T_d"] = dt("xn2T_d", [DM, S], BF16, kind="Internal").ap()
    T["x3T_d"] = dt("x3T_d", [DFFP, S], BF16, kind="Internal").ap()
    T["y3_d"] = dt("y3_d", [S, DM], F32, kind="Internal").ap()
    T["cosin_d"] = dt("cosin_d", [S, 64], F32, kind="Internal").ap()
    T["cosrep_d"] = dt("cosrep_d", [S, 1024], F32, kind="Internal").ap()
    T["bar_i"] = dt("bar_i", [1, 8], F32, kind="Internal").ap()
    T["bar_o"] = dt("bar_o", [1, 8], F32, kind="Internal").ap()

    dbg_out = {}
    if any(d.startswith("stop_a") for d in dbg):
        dbg_out["dbg_stage"] = dt("dbg_stage", [128, 32768], F32, kind="ExternalOutput").ap()
    for name, shape, ddt in [("dbg_wg", [DM, DFFP], BF16), ("dbg_wd", [8 * D_PAD, DM], BF16),
                             ("dbg_wgsh", [512, DFF], BF16),
                             ("dbg_x1", [S, DM], F32), ("dbg_xn2T", [DM, S], BF16),
                             ("dbg_x3T", [DFFP, S], BF16), ("dbg_y3", [S, DM], F32),
                             ("dbg_cosrep", [S, 1024], F32)]:
        if name in dbg:
            dbg_out[name] = dt(name, shape, ddt, kind="ExternalOutput").ap()

    ident = st("ident", [128, 128], BF16).__enter__()
    identf = st("identf", [128, 128], F32).__enter__()
    caus = st("caus", [128, 128], F32).__enter__()
    s16 = st("s16", [128, 64], BF16).__enter__()
    s4 = st("s4", [128, 64], BF16).__enter__()
    biases = st("biases", [128, 20], F32).__enter__()
    ktb = st("ktb", [128, 1], F32).__enter__()
    cvals = st("cvals", [128, 4], F32).__enter__()
    lora_a = st("lora_a", [128, 150 * R], BF16).__enter__()

    pb = [nc.psum_tensor(f"pb{i}", [128, 512], F32).__enter__() for i in range(6)]
    pt = [nc.psum_tensor(f"pt{i}", [128, 512], BF16).__enter__() for i in range(2)]

    blk_cm = nc.Block()
    block = blk_cm.__enter__()
    E = Env(nc, block)
    E.dbg = dbg
    E.T = T
    E.dbg_out = dbg_out
    E.pb, E.pt = pb, pt
    E.consts = dict(ident=ident, identf=identf, caus=caus, s16=s16, s4=s4, biases=biases, ktb=ktb,
                    cvals=cvals, lora_a=lora_a)

    _emit_prologue(E)
    if ffn:
        _emit_dequant(E)
    if attn:
        _emit_attention(E)
    else:
        _emit_ln2_standalone(E)

    if ffn:
        with st("lorab", [128, DFFP], BF16) as lorab:
            E.consts["lorab"] = lorab

            @block.gpsimd
            def _(g):
                g.dma_start(out=lorab[0:16, 0:DFFP], in_=T["glb"][:]).then_inc(E.gp.h, 16)
                g.dma_start(out=lorab[32:48, 0:DFFP], in_=T["ulb"][:]).then_inc(E.gp.h, 16)
                g.dma_start(out=lorab[64:80, 0:DM], in_=T["dlb"][:]).then_inc(E.gp.h, 16)
                E.gp.n += 48
            E.mark["lorab_gp"] = E.gp.n
            _emit_f1(E)
            _emit_f2(E)
            _emit_f3(E)
    else:
        @block.sync
        def _(sy):
            sy.wait_ge(E.sy.h, E.sy.n)
            sy.wait_ge(E.gp.h, E.gp.n)
            sy.wait_ge(E.pe.h, E.pe.n)
            for tt in range(NT):
                sy.dma_start(out=T["out_t"][tt * 128:(tt + 1) * 128, :],
                             in_=T["x1_d"][tt * 128:(tt + 1) * 128, :]).then_inc(E.st.h, 16)
                E.st.n += 16

    for nm, src in [("dbg_wg", T["wg_f"]), ("dbg_wd", T["wd_f"]),
                    ("dbg_cosrep", T["cosrep_d"]),
                    ("dbg_wgsh", T["wg_sh"]), ("dbg_x1", T["x1_d"]),
                    ("dbg_xn2T", T["xn2T_d"]), ("dbg_x3T", T["x3T_d"]),
                    ("dbg_y3", T["y3_d"])]:
        if nm in dbg_out:
            _emit_dram_copy(E, dbg_out[nm], src)

    @block.sync
    def _(sy):
        sy.wait_ge(E.st.h, E.st.n)

    blk_cm.__exit__(None, None, None)
    return nc


_DBG_N = [0]


def _emit_dram_copy(E, dst, src):
    block = E.block
    nrow, ncol = src.shape[0], src.shape[1]
    _DBG_N[0] += 1
    with E.nc.sbuf_tensor(f"dbgcp{_DBG_N[0]}", [128, ncol], src.dtype) as tb:
        @block.sync
        def _(sy):
            for ctr in (E.sy, E.gp, E.sc, E.pe, E.ve, E.ac, E.cc):
                if ctr.n:
                    sy.wait_ge(ctr.h, ctr.n)
            for rr in range(0, nrow, 128):
                sy.dma_start(out=tb[:], in_=src[rr:rr + 128, :]).then_inc(E.sy.h, 16)
                E.sy.n += 16
                sy.wait_ge(E.sy.h, E.sy.n)
                sy.dma_start(out=dst[rr:rr + 128, :], in_=tb[:]).then_inc(E.st.h, 16)
                E.st.n += 16
                sy.wait_ge(E.st.h, E.st.n)


def _emit_prologue(E):
    nc, block, T, C = E.nc, E.block, E.T, E.consts
    with nc.sbuf_tensor("iotai", [128, 128], I32) as iotai:
        @block.gpsimd
        def _(g):
            fence(g, E.gp, g.iota(iotai[:], [[1, 128]], base=0, channel_multiplier=-1))

        @block.vector
        def _(v):
            v.wait_ge(E.gp.h, E.gp.n)
            v.tensor_scalar(C["ident"][:], iotai[:], 0.0, None, ALU.is_equal)
            v.tensor_scalar(C["identf"][:], iotai[:], 0.0, None, ALU.is_equal)
            fence(v, E.ve, v.tensor_scalar(C["caus"][:], iotai[:], 0.0, -1e30, ALU.is_gt, ALU.mult))

        @block.gpsimd
        def _(g):
            g.wait_ge(E.ve.h, E.ve.n)
            fence(g, E.gp, g.iota(iotai[:, 0:64], [[-1, 64]], base=0, channel_multiplier=1))

        @block.vector
        def _(v):
            v.wait_ge(E.gp.h, E.gp.n)
            v.tensor_scalar(C["s16"][:], iotai[:, 0:64], 0.0, None, ALU.is_equal)
            fence(v, E.ve, v.tensor_scalar(C["s4"][:], iotai[:, 0:64], 64.0, None, ALU.is_equal))
            fence(v, E.ve, v.tensor_tensor(C["s16"][:], C["s16"][:], C["s4"][:], ALU.add))
            v.tensor_scalar(C["s4"][:], C["s16"][:], 0.25, None, ALU.mult)
            fence(v, E.ve, v.tensor_scalar(C["s16"][:], C["s16"][:], 1.0 / 16.0, None, ALU.mult))

    @block.vector
    def _(v):
        import math
        v.memset(C["cvals"][:, 0:1], math.pi / 2)
        fence(v, E.ve, v.memset(C["cvals"][:, 1:2], math.log(0.125)))

    @block.sync
    def _(sy):
        sy.dma_start(out=C["biases"][:, 0:8], in_=T["qc1b"][:]).then_inc(E.sy.h, 16)
        sy.dma_start(out=C["biases"][:, 8:16], in_=T["qc2b"][:]).then_inc(E.sy.h, 16)
        sy.dma_start(out=C["biases"][:, 16:18], in_=T["kc1b"][:]).then_inc(E.sy.h, 16)
        sy.dma_start(out=C["biases"][:, 18:20], in_=T["kc2b"][:]).then_inc(E.sy.h, 16)
        sy.dma_start(out=C["ktb"][:], in_=T["ktemp"].to_broadcast([128, 1])).then_inc(E.sy.h, 16)
        E.sy.n += 80

    @block.gpsimd
    def _(g):
        ofs = 0
        for src, nch in [(T["gla"], 32), (T["ula"], 32), (T["dla"], 86)]:
            for ch in range(nch):
                g.dma_start(out=C["lora_a"][:, (ofs + ch) * R:(ofs + ch + 1) * R],
                            in_=src[ch * 128:(ch + 1) * 128, :]).then_inc(E.gp.h, 16)
                E.gp.n += 16
            ofs += nch
    E.mark["lora_gp"] = E.gp.n


def _emit_dequant(E):
    nc, block, T = E.nc, E.block, E.T
    JC = 2048
    mats = [(T["gcodes"], T["gabs"], T["wg_sh"], T["wg_f"], 4, DFFP),
            (T["ucodes"], T["uabs"], T["wu_sh"], T["wu_f"], 4, DFFP),
            (T["dcodes"], T["dabs"], T["wd_sh"], T["wd_f"], 11, DM)]
    with (nc.sbuf_tensor("dq_c", [128, 2 * DFFP], BF16) as cbuf,
          nc.sbuf_tensor("dq_m", [128, DFFP], F32) as mbuf,
          nc.sbuf_tensor("dq_b3", [128, JC], I8) as b3,
          nc.sbuf_tensor("dq_b2", [128, JC], I8) as b2,
          nc.sbuf_tensor("dq_b1", [128, JC], I8) as b1,
          nc.sbuf_tensor("dq_b0", [128, JC], BF16) as b0,
          nc.sbuf_tensor("dq_r", [128, JC], BF16) as rr_,
          nc.sbuf_tensor("dq_f", [128, 8 * JC], BF16) as fbuf,
          nc.sbuf_tensor("dq_bar", [1, 8], F32) as bmbuf):
        F = lambda k, w: fbuf[:, k * JC:k * JC + w]
        ext_done = {}
        mult_done = 0
        store_done = 0
        ti = 0

        for mi, (codes, absm, shard, full, ntile, J) in enumerate(mats):
            nchunk = (J + JC - 1) // JC
            for i in range(ntile):
                par = ti % 2
                gate = ext_done.get(ti - 2, 0)

                @block.gpsimd
                def _(g, i=i, par=par, codes=codes, J=J, gate=gate):
                    if gate:
                        g.wait_ge(E.ve.h, gate)
                    g.dma_start(out=cbuf[:, par * DFFP:par * DFFP + J],
                                in_=codes[i * 128:(i + 1) * 128, :]).then_inc(E.gp.h, 16)
                    E.gp.n += 16
                ld_codes = E.gp.n

                @block.sync
                def _(sy, i=i, absm=absm, J=J, mult_done=mult_done):
                    if mult_done:
                        sy.wait_ge(E.ve.h, mult_done)
                    sy.dma_start(out=mbuf[0:64, 0:J],
                                 in_=absm[2 * i:2 * i + 1, :].to_broadcast([64, J]))\
                      .then_inc(E.sy.h, 16)
                    sy.dma_start(out=mbuf[64:128, 0:J],
                                 in_=absm[2 * i + 1:2 * i + 2, :].to_broadcast([64, J]))\
                      .then_inc(E.sy.h, 16)
                    E.sy.n += 32
                ld_abs = E.sy.n

                for jc in range(nchunk):
                    j0 = jc * JC
                    w = min(JC, J - j0)
                    Cc = cbuf[:, par * DFFP + j0: par * DFFP + j0 + w]

                    @block.vector
                    def _(v, Cc=Cc, w=w, ld_codes=ld_codes):
                        v.wait_ge(E.gp.h, ld_codes)
                        v.tensor_scalar(b3[:, 0:w], Cc, 8.0, None, ALU.is_ge)
                        v.scalar_tensor_tensor(rr_[:, 0:w], b3[:, 0:w], -8.0, Cc, ALU.mult, ALU.add)
                        v.tensor_scalar(b2[:, 0:w], rr_[:, 0:w], 4.0, None, ALU.is_ge)
                        v.scalar_tensor_tensor(rr_[:, 0:w], b2[:, 0:w], -4.0, rr_[:, 0:w], ALU.mult, ALU.add)
                        v.tensor_scalar(b1[:, 0:w], rr_[:, 0:w], 2.0, None, ALU.is_ge)
                        inst = v.scalar_tensor_tensor(b0[:, 0:w], b1[:, 0:w], -2.0, rr_[:, 0:w], ALU.mult, ALU.add)
                        fence(v, E.ve, inst)
                    if jc == nchunk - 1:
                        ext_done[ti] = E.ve.n

                    @block.vector
                    def _(v, w=w, j0=j0, ld_abs=ld_abs, store_done=store_done):
                        if store_done:
                            v.wait_ge(E.gp.h, store_done)
                        for k in range(8):
                            lo, hi = float(NF4[2 * k]), float(NF4[2 * k + 1])
                            inst = v.tensor_scalar(F(k, w), b0[:, 0:w], hi - lo, lo, ALU.mult, ALU.add)
                        fence(v, E.ve, inst)
                        v.copy_predicated(F(0, w), b1[:, 0:w], F(1, w))
                        v.copy_predicated(F(2, w), b1[:, 0:w], F(3, w))
                        v.copy_predicated(F(4, w), b1[:, 0:w], F(5, w))
                        fence(v, E.ve, v.copy_predicated(F(6, w), b1[:, 0:w], F(7, w)))
                        v.copy_predicated(F(0, w), b2[:, 0:w], F(2, w))
                        fence(v, E.ve, v.copy_predicated(F(4, w), b2[:, 0:w], F(6, w)))
                        fence(v, E.ve, v.copy_predicated(F(0, w), b3[:, 0:w], F(4, w)))
                        v.wait_ge(E.sy.h, ld_abs)
                        inst = v.tensor_tensor(F(0, w), F(0, w), mbuf[:, j0:j0 + w], ALU.mult)
                        fence(v, E.ve, inst)
                    mult_done = E.ve.n

                    @block.gpsimd
                    def _(g, i=i, j0=j0, w=w, shard=shard):
                        g.wait_ge(E.ve.h, E.ve.n)
                        g.dma_start(out=shard[i * 128:(i + 1) * 128, j0:j0 + w],
                                    in_=F(0, w)).then_inc(E.gp.h, 16)
                        E.gp.n += 16
                    store_done = E.gp.n
                ti += 1

        @block.gpsimd
        def _(g):
            g.wait_ge(E.gp.h, E.gp.n)
            g.wait_ge(E.sy.h, E.sy.n)
            g.wait_ge(E.ve.h, E.ve.n)
            for (codes, absm, shard, full, ntile, J) in mats:
                g.collective_compute("AllGather", ALU.bypass,
                                     replica_groups=[list(range(8))],
                                     ins=[shard.opt()], outs=[full.opt()]).then_inc(E.cc.h, 1)
                E.cc.n += 1
        E.mark["gather0"] = E.mark["gather1"] = E.mark["gather2"] = E.cc.n

        @block.gpsimd
        def _(g):
            g.memset(bmbuf[:], 1.0)
            fence(g, E.gp, g.engine_nop())
            g.dma_start(out=T["bar_i"][:], in_=bmbuf[:]).then_inc(E.gp.h, 16)
            E.gp.n += 16
            g.wait_ge(E.gp.h, E.gp.n)
            g.collective_compute("AllReduce", ALU.add,
                                 replica_groups=[list(range(8))],
                                 ins=[T["bar_i"].opt()], outs=[T["bar_o"].opt()]).then_inc(E.cc.h, 1)
            E.cc.n += 1
        E.mark["wbarrier"] = E.cc.n


def _emit_ln2_standalone(E):
    nc, block, T = E.nc, E.block, E.T
    ident, pt = E.consts["ident"], E.pt
    with (nc.sbuf_tensor("l2_x", [128, DM], F32) as xb,
          nc.sbuf_tensor("l2_scr", [128, DM], F32) as scr,
          nc.sbuf_tensor("l2_st", [128, 4], F32) as stat,
          nc.sbuf_tensor("l2_n", [128, DM], BF16) as nb,
          nc.sbuf_tensor("l2_t", [128, DM], BF16) as tb):
        for tt in range(NT):
            @block.sync
            def _(sy, tt=tt):
                if E.cc.n:
                    sy.wait_ge(E.cc.h, E.cc.n)
                sy.wait_ge(E.gp.h, E.gp.n)
                sy.wait_ge(E.pe.h, E.pe.n)
                sy.dma_start(out=xb[:], in_=T["x1_d"][tt * 128:(tt + 1) * 128, :]).then_inc(E.sy.h, 16)
                E.sy.n += 16

            @block.vector
            def _(v):
                v.wait_ge(E.sy.h, E.sy.n)
            _ln_core(E, xb, scr, stat, DM)

            @block.vector
            def _(v):
                v.wait_ge(E.ac.h, E.ac.n)
                fence(v, E.ve, v.tensor_scalar(nb[:], xb[:], stat[:, 3:4], None, ALU.mult))

            for grp in range(8):
                bank = pt[grp % 2]

                @block.tensor
                def _(t, grp=grp, bank=bank):
                    t.wait_ge(E.ve.h, E.ve.n)
                    for k in range(4):
                        ch = grp * 4 + k
                        inst = t.transpose(bank[:, k * 128:(k + 1) * 128],
                                           nb[:, ch * 128:(ch + 1) * 128], ident[:])
                    E.pe.inc(inst)

                @block.vector
                def _(v, grp=grp, bank=bank):
                    v.wait_ge(E.pe.h, E.pe.n)
                    fence(v, E.ve, v.tensor_copy(tb[:, grp * 512:(grp + 1) * 512], bank[:]))

            @block.gpsimd
            def _(g, tt=tt):
                g.wait_ge(E.ve.h, E.ve.n)
                for ch in range(32):
                    g.dma_start(out=T["xn2T_d"][ch * 128:(ch + 1) * 128, tt * 128:(tt + 1) * 128],
                                in_=tb[:, ch * 128:(ch + 1) * 128]).then_inc(E.gp.h, 16)
                    E.gp.n += 16
    E.mark["xn2T_gp"] = E.gp.n
    E.mark["xn2T_sy"] = 0
    E.mark["x1_gp"] = 0


def _scope_barrier(E):
    block = E.block

    @block.sync
    def _(sy):
        for ctr in (E.pe, E.ve, E.ac, E.gp, E.sc):
            if ctr.n:
                sy.wait_ge(ctr.h, ctr.n)

    @block.gpsimd
    def _(g):
        for ctr in (E.pe, E.ve, E.ac, E.sy, E.sc):
            if ctr.n:
                g.wait_ge(ctr.h, ctr.n)

    @block.scalar
    def _(s):
        for ctr in (E.pe, E.ve, E.gp, E.sy, E.sc):
            if ctr.n:
                s.wait_ge(ctr.h, ctr.n)


def _dump_sbuf(E, bufs):
    block = E.block
    dst = E.dbg_out["dbg_stage"]
    col = [0]

    @block.sync
    def _(sy):
        for ctr in (E.sy, E.gp, E.sc, E.pe, E.ve, E.ac):
            if ctr.n:
                sy.wait_ge(ctr.h, ctr.n)
    @block.gpsimd
    def _(g):
        for ctr in (E.sy, E.pe, E.ve, E.ac):
            if ctr.n:
                g.wait_ge(ctr.h, ctr.n)
        for b in bufs:
            w = b.shape[-1]
            npart = b.shape[0]
            g.dma_start(out=dst[0:npart, col[0]:col[0] + w], in_=b).then_inc(E.st.h, 16)
            E.st.n += 16
            col[0] += w


def _emit_attention(E):
    import math
    nc, block, T, C = E.nc, E.block, E.T, E.consts
    pb, pt = E.pb, E.pt
    ident, identf, caus, cvals = C["ident"], C["identf"], C["caus"], C["cvals"]
    biases, ktb = C["biases"], C["ktb"]

    with (nc.sbuf_tensor("a0_i", [128, 32], I32, side="right") as ioti,
          nc.sbuf_tensor("a0_f", [128, 32], F32, side="right") as invb,
          nc.sbuf_tensor("a0_t", [128, 8], F32, side="right") as tval,
          nc.sbuf_tensor("a0_w", [128, 6 * 32], F32, side="right") as w0):
        ANG, KF, M2, AB, CO, SI = (w0[:, i * 32:(i + 1) * 32] for i in range(6))

        @block.gpsimd
        def _(g):
            fence(g, E.gp, g.iota(ioti[:, 0:32], [[1, 32]], base=0, channel_multiplier=0))

        @block.vector
        def _(v):
            v.wait_ge(E.gp.h, E.gp.n)
            fence(v, E.ve, v.tensor_scalar(invb[:], ioti[:, 0:32], 1.0, None, ALU.bypass))

        @block.scalar
        def _(s):
            s.wait_ge(E.ve.h, E.ve.n)
            fence(s, E.ac, s.activation(invb[:], invb[:], AF.Exp, scale=-math.log(10000.0) / 32.0))

        for tt in range(NT):
            @block.gpsimd
            def _(g, tt=tt):
                g.wait_ge(E.sy.h, E.sy.n)
                fence(g, E.gp, g.iota(tval[:, 0:1], [[1, 1]], base=128 * tt,
                                      channel_multiplier=1,
                                      allow_small_or_imprecise_dtypes=True))

            @block.vector
            def _(v):
                v.wait_ge(E.gp.h, E.gp.n)
                v.wait_ge(E.ac.h, E.ac.n)
                fence(v, E.ve, v.tensor_scalar(ANG, invb[:], tval[:, 0:1], None, ALU.mult))
                fence(v, E.ve, v.tensor_scalar(ioti[:, 0:32], ANG, 1.0 / (2 * math.pi), None, ALU.mult))
                fence(v, E.ve, v.tensor_scalar(KF, ioti[:, 0:32], 1.0, None, ALU.bypass))
                fence(v, E.ve, v.scalar_tensor_tensor(M2, KF, -2 * math.pi, ANG, ALU.mult, ALU.add))
                fence(v, E.ve, v.tensor_scalar(AB, M2, math.pi, None, ALU.is_ge))
                fence(v, E.ve, v.scalar_tensor_tensor(M2, AB, -2 * math.pi, M2, ALU.mult, ALU.add))
                fence(v, E.ve, v.tensor_scalar(AB, M2, -math.pi, None, ALU.is_lt))
                fence(v, E.ve, v.scalar_tensor_tensor(M2, AB, 2 * math.pi, M2, ALU.mult, ALU.add))
                fence(v, E.ve, v.tensor_scalar(M2, M2, math.pi, -math.pi, ALU.min, ALU.max))

            @block.scalar
            def _(s):
                s.wait_ge(E.ve.h, E.ve.n)
                fence(s, E.ac, s.activation(SI, M2, AF.Sin))
                fence(s, E.ac, s.activation(AB, M2, AF.Abs))
                fence(s, E.ac, s.activation(CO, AB, AF.Sin, scale=-1.0, bias=cvals[:, 0:1]))

            @block.sync
            def _(sy, tt=tt):
                sy.wait_ge(E.ac.h, E.ac.n)
                for h in range(16):
                    sy.dma_start(out=T["cosrep_d"][tt * 128:(tt + 1) * 128, h * 32:(h + 1) * 32],
                                 in_=CO).then_inc(E.sy.h, 16)
                    sy.dma_start(out=T["cosrep_d"][tt * 128:(tt + 1) * 128, 512 + h * 32:512 + (h + 1) * 32],
                                 in_=SI).then_inc(E.sy.h, 16)
                E.sy.n += 32 * 16

    with (nc.sbuf_tensor("at_qpreT", [128, 8 * 2052], BF16) as qpreT,
          nc.sbuf_tensor("at_kpreT", [128, 2 * 2052], BF16) as kpreT,
          nc.sbuf_tensor("at_vfm", [128, 2 * 2048], BF16) as vfm,
          nc.sbuf_tensor("at_qm", [128, NT * 64], F32) as qmean,
          nc.sbuf_tensor("at_km", [128, NT * 64], F32) as kmean,
          nc.sbuf_tensor("at_st", [128, 8], F32) as stat):

        QP = lambda ch: qpreT[:, ch * 2052:(ch + 1) * 2052]
        KP = lambda ch: kpreT[:, ch * 2052:(ch + 1) * 2052]

        @block.vector
        def _(v):
            for ch in range(8):
                v.memset(QP(ch)[:, 0:2], 0.0)
            for ch in range(2):
                v.memset(KP(ch)[:, 0:2], 0.0)
            v.memset(vfm[:, 2048:2049], 0.0)
            fence(v, E.ve, v.engine_nop())

        _scope_barrier(E)
        with (nc.sbuf_tensor("a1_w", [128, 32 * 512], BF16, side="right") as wqkv,
              nc.sbuf_tensor("a1_vw", [128, 2 * 32 * 128], BF16, side="right") as vwb,
              nc.sbuf_tensor("a1_x", [128, DM], F32, side="right") as xb,
              nc.sbuf_tensor("a1_n", [128, DM], BF16, side="right") as nb,
              nc.sbuf_tensor("a1_xt", [128, 32 * 512], BF16, side="right") as xnT):
            @block.gpsimd
            def _(g):
                for ch in range(32):
                    g.dma_start(out=wqkv[:, ch * 512:ch * 512 + 512],
                                in_=T["wqkvT"][ch * 128:(ch + 1) * 128, 0:512]).then_inc(E.gp.h, 16)
                    E.gp.n += 16
            wq_done = E.gp.n

            def load_oc_slice(g, oc, gate):
                if gate:
                    g.wait_ge(E.pe.h, gate)
                slot = oc % 2
                g.dma_start(out=vwb[:, slot * 4096:(slot + 1) * 4096],
                            in_=T["wqkvT"][:, oc * 128:(oc + 1) * 128]
                            .rearrange("(c p) j -> p c j", p=128)).then_inc(E.gp.h, 16)
                E.gp.n += 16
                return E.gp.n

            ev_oc = []
            for tg in range(4):
                t0 = tg * 512
                for sub in range(4):
                    tt = tg * 4 + sub

                    @block.sync
                    def _(sy, tt=tt):
                        sy.wait_ge(E.pe.h, E.pe.n)
                        sy.dma_start(out=xb[:], in_=T["x_in"][tt * 128:(tt + 1) * 128, :]).then_inc(E.sy.h, 16)
                        E.sy.n += 16

                    @block.vector
                    def _(v):
                        v.wait_ge(E.sy.h, E.sy.n)
                    _ln_core(E, xb, nb, stat, DM)

                    @block.vector
                    def _(v):
                        v.wait_ge(E.ac.h, E.ac.n)
                        fence(v, E.ve, v.tensor_scalar(nb[:], xb[:], stat[:, 3:4], None, ALU.mult))

                    for grp in range(8):
                        bank = pt[grp % 2]

                        @block.tensor
                        def _(t, grp=grp, bank=bank):
                            t.wait_ge(E.ve.h, E.ve.n)
                            for k in range(4):
                                ch = grp * 4 + k
                                inst = t.transpose(bank[:, k * 128:(k + 1) * 128],
                                                   nb[:, ch * 128:(ch + 1) * 128], ident[:])
                            E.pe.inc(inst)

                        @block.vector
                        def _(v, grp=grp, bank=bank, sub=sub):
                            v.wait_ge(E.pe.h, E.pe.n)
                            for k in range(4):
                                ch = grp * 4 + k
                                inst = v.tensor_copy(xnT[:, ch * 512 + sub * 128:ch * 512 + sub * 128 + 128],
                                                     bank[:, k * 128:(k + 1) * 128])
                            fence(v, E.ve, inst)
                xn_done = E.ve.n

                ld_ok = {}
                @block.gpsimd
                def _(g, gate=E.mark.get("a1_strm_pe", 0)):
                    ld_ok[4] = load_oc_slice(g, 4, gate)
                    ld_ok[5] = load_oc_slice(g, 5, gate)

                pe_oc = {}
                for oc in range(12):
                    bank = pb[oc % 4]
                    gate_ac = ev_oc[-4] if len(ev_oc) >= 4 else 0

                    @block.tensor
                    def _(t, oc=oc, bank=bank, xn_done=xn_done, wq_done=wq_done,
                          gate_ac=gate_ac, lo=ld_ok.get(oc, 0)):
                        t.wait_ge(E.ve.h, xn_done)
                        t.wait_ge(E.gp.h, wq_done if oc < 4 else lo)
                        if gate_ac:
                            t.wait_ge(E.ac.h, gate_ac)
                        for ch in range(32):
                            if oc < 4:
                                lw = wqkv[:, ch * 512 + oc * 128:ch * 512 + (oc + 1) * 128]
                            else:
                                lw = vwb[:, (oc % 2) * 4096 + ch * 128:(oc % 2) * 4096 + (ch + 1) * 128]
                            inst = t.matmul(bank[:], lw, xnT[:, ch * 512:(ch + 1) * 512],
                                            start=(ch == 0), stop=(ch == 31))
                        E.pe.inc(inst)
                    pe_oc[oc] = E.pe.n
                    if 4 <= oc <= 9:
                        @block.gpsimd
                        def _(g, oc=oc, gate=pe_oc[oc]):
                            ld_ok[oc + 2] = load_oc_slice(g, oc + 2, gate)

                    @block.scalar
                    def _(s, oc=oc, bank=bank, t0=t0):
                        s.wait_ge(E.pe.h, E.pe.n)
                        if oc < 8:
                            inst = s.activation(QP(oc)[:, 2 + t0:2 + t0 + 512], bank[:], AF.Copy)
                        elif oc < 10:
                            inst = s.activation(KP(oc - 8)[:, 2 + t0:2 + t0 + 512], bank[:], AF.Copy)
                        elif oc == 10:
                            inst = s.activation(vfm[:, t0:t0 + 512], bank[:], AF.Copy)
                        else:
                            w = 512 if t0 + 512 < S else 511
                            inst = s.activation(vfm[:, 2048 + 1 + t0:2048 + 1 + t0 + w],
                                                bank[:, 0:w], AF.Copy)
                        E.ac.inc(inst)
                    ev_oc.append(E.ac.n)
                E.mark["a1_strm_pe"] = E.pe.n

                @block.tensor
                def _(t, t0=t0):
                    t.wait_ge(E.ac.h, E.ac.n)
                    t.wait_ge(E.ve.h, E.ve.n)
                    for ch in range(8):
                        t.matmul(pb[4][0:64, :], C["s16"][:], QP(ch)[:, 2 + t0:2 + t0 + 512],
                                 start=(ch == 0), stop=(ch == 7))
                    for ch in range(2):
                        inst = t.matmul(pb[5][0:64, :], C["s4"][:], KP(ch)[:, 2 + t0:2 + t0 + 512],
                                        start=(ch == 0), stop=(ch == 1))
                    E.pe.inc(inst)

                @block.vector
                def _(v):
                    v.wait_ge(E.pe.h, E.pe.n)
                    v.tensor_copy(xb[0:64, 0:512], pb[4][0:64, :])
                    fence(v, E.ve, v.tensor_copy(xb[0:64, 512:1024], pb[5][0:64, :]))

                for sub in range(4):
                    @block.tensor
                    def _(t, sub=sub):
                        t.wait_ge(E.ve.h, E.ve.n)
                        dv = pt[0].bitcast(F32)
                        t.transpose(dv[:, 0:64], xb[0:64, sub * 128:(sub + 1) * 128], identf[0:64, 0:64])
                        inst = t.transpose(dv[:, 64:128], xb[0:64, 512 + sub * 128:512 + (sub + 1) * 128],
                                           identf[0:64, 0:64])
                        E.pe.inc(inst)

                    @block.vector
                    def _(v, sub=sub, tg=tg):
                        v.wait_ge(E.pe.h, E.pe.n)
                        tt = tg * 4 + sub
                        sv = pt[0].bitcast(F32)
                        v.tensor_copy(qmean[:, tt * 64:(tt + 1) * 64], sv[:, 0:64])
                        inst = v.tensor_copy(kmean[:, tt * 64:(tt + 1) * 64], sv[:, 64:128])
                        fence(v, E.ve, inst)

        if "stop_a1" in E.dbg:
            _dump_sbuf(E, [qpreT[:], kpreT[:], vfm[:], qmean[:], kmean[:]])
            return

        _scope_barrier(E)
        with (nc.sbuf_tensor("a2_c1q", [128, 8 * 2052], BF16, side="right") as c1q,
              nc.sbuf_tensor("a2_c1k", [128, 2 * 2052], BF16, side="right") as c1k,
              nc.sbuf_tensor("a2_w1", [128, 30 * 64], BF16, side="right") as w1,
              nc.sbuf_tensor("a2_w2", [128, 3 * 8 * 1024 + 3 * 2 * 256], BF16, side="right") as w2):
            C1Q = lambda ch: c1q[:, ch * 2052:(ch + 1) * 2052]
            C1K = lambda ch: c1k[:, ch * 2052:(ch + 1) * 2052]
            KW2 = 3 * 8 * 1024

            @block.vector
            def _(v):
                for ch in range(8):
                    v.memset(C1Q(ch)[:, 0:2], 0.0)
                for ch in range(2):
                    v.memset(C1K(ch)[:, 0:2], 0.0)
                fence(v, E.ve, v.engine_nop())

            @block.gpsimd
            def _(g):
                for tap in range(3):
                    for gg in range(16):
                        p, r0 = gg // 2, (gg % 2) * 64
                        g.dma_start(out=w1[r0:r0 + 64, (tap * 8 + p) * 64:(tap * 8 + p + 1) * 64],
                                    in_=T["qc1T"][tap, gg, :, :]).then_inc(E.gp.h, 16)
                        E.gp.n += 16
                    for gg in range(4):
                        p, r0 = gg // 2, (gg % 2) * 64
                        g.dma_start(out=w1[r0:r0 + 64, (24 + tap * 2 + p) * 64:(24 + tap * 2 + p + 1) * 64],
                                    in_=T["kc1T"][tap, gg, :, :]).then_inc(E.gp.h, 16)
                        E.gp.n += 16
                    for ic in range(8):
                        g.dma_start(out=w2[:, (tap * 8 + ic) * 1024:(tap * 8 + ic + 1) * 1024],
                                    in_=T["qc2T"][tap, ic * 128:(ic + 1) * 128, :]).then_inc(E.gp.h, 16)
                        E.gp.n += 16
                    for ic in range(2):
                        g.dma_start(out=w2[:, KW2 + (tap * 2 + ic) * 256:KW2 + (tap * 2 + ic + 1) * 256],
                                    in_=T["kc2T"][tap, ic * 128:(ic + 1) * 128, :]).then_inc(E.gp.h, 16)
                        E.gp.n += 16
            w_conv = E.gp.n

            for (npair, SRC, DST, coff, bbase, ntap) in ((8, QP, C1Q, 0, 0, 8), (2, KP, C1K, 24, 16, 2)):
                for tg in range(4):
                    t0 = tg * 512
                    for p in range(npair):
                        bank = pb[(tg * npair + p) % 4]

                        @block.tensor
                        def _(t, p=p, t0=t0, bank=bank, SRC=SRC, coff=coff, ntap=ntap, w_conv=w_conv):
                            t.wait_ge(E.gp.h, w_conv)
                            t.wait_ge(E.ac.h, E.ac.n)
                            for tap in range(3):
                                wc = (coff + tap * ntap + p) * 64
                                rhs = SRC(p)[:, t0 + tap:t0 + tap + 512]
                                t.matmul(bank[0:64, :], w1[0:64, wc:wc + 64], rhs[0:64, :],
                                         start=(tap == 0), stop=(tap == 2))
                            for tap in range(3):
                                wc = (coff + tap * ntap + p) * 64
                                rhs = SRC(p)[:, t0 + tap:t0 + tap + 512]
                                inst = t.matmul(bank[64:128, :], w1[64:128, wc:wc + 64], rhs[64:128, :],
                                                start=(tap == 0), stop=(tap == 2),
                                                tile_position=(64, 64))
                            E.pe.inc(inst)

                        @block.vector
                        def _(v, p=p, t0=t0, bank=bank, DST=DST, bbase=bbase):
                            v.wait_ge(E.pe.h, E.pe.n)
                            inst = v.tensor_scalar(DST(p)[:, 2 + t0:2 + t0 + 512], bank[:],
                                                   biases[:, bbase + p:bbase + p + 1], None, ALU.add)
                            E.ac.inc(inst)

            for (nic, SRC, DST, w2off, bbase) in ((8, C1Q, QP, 0, 8), (2, C1K, KP, KW2, 18)):
                wid = nic * 128
                for tg in range(4):
                    t0 = tg * 512
                    for oc in range(nic):
                        bank = pb[(tg * nic + oc) % 4]

                        @block.tensor
                        def _(t, oc=oc, t0=t0, bank=bank, nic=nic, SRC=SRC, w2off=w2off, wid=wid):
                            t.wait_ge(E.ac.h, E.ac.n)
                            for tap in range(3):
                                for ic in range(nic):
                                    rhs = SRC(ic)[:, t0 + tap:t0 + tap + 512]
                                    inst = t.matmul(bank[:],
                                                    w2[:, w2off + (tap * nic + ic) * wid + oc * 128:
                                                       w2off + (tap * nic + ic) * wid + oc * 128 + 128],
                                                    rhs, start=(tap == 0 and ic == 0),
                                                    stop=(tap == 2 and ic == nic - 1))
                            E.pe.inc(inst)

                        @block.vector
                        def _(v, oc=oc, t0=t0, bank=bank, DST=DST, bbase=bbase):
                            v.wait_ge(E.pe.h, E.pe.n)
                            inst = v.tensor_scalar(DST(oc)[:, 2 + t0:2 + t0 + 512], bank[:],
                                                   biases[:, bbase + oc:bbase + oc + 1], None, ALU.add)
                            E.ac.inc(inst)
        conv_ac = E.ac.n
        if "stop_a2" in E.dbg:
            _dump_sbuf(E, [qpreT[:], kpreT[:]])
            return

        _scope_barrier(E)
        qt_cms = [nc.sbuf_tensor("a3_QT", [128, 8 * 2048], BF16, side="right"),
                  nc.sbuf_tensor("a3_KT", [128, 2 * 2048], BF16, side="right"),
                  nc.sbuf_tensor("a3_V", [128, NT * 256], BF16, side="right")]
        QT = qt_cms[0].__enter__()
        KT = qt_cms[1].__enter__()
        VT = qt_cms[2].__enter__()
        with (nc.sbuf_tensor("a3_cs", [128, 1024], F32, side="right") as csb,
              nc.sbuf_tensor("a3_tm", [128, 1024], BF16, side="right") as tmb,
              nc.sbuf_tensor("a3_qo", [128, 1024 + 256], BF16, side="right") as qko,
              nc.sbuf_tensor("a3_wk", [128, 3200], F32, side="right") as wk,
              nc.sbuf_tensor("a3_ss", [128, 48], F32, side="right") as ssb):
            KH = wk[:, 0:64]
            QH = wk[:, 64:128]
            SQ = wk[:, 128:1152]
            T1 = wk[:, 1152:1664]
            T2 = wk[:, 1664:2176]
            QW = wk[:, 2176:3200]

            a3lim = NT
            for d in E.dbg:
                if d.startswith("a3lim"):
                    a3lim = int(d[5:])
            for tt in range(a3lim):
                t0 = tt * 128

                @block.sync
                def _(sy, tt=tt):
                    sy.wait_ge(E.ve.h, E.ve.n)
                    sy.dma_start(out=csb[:], in_=T["cosrep_d"][tt * 128:(tt + 1) * 128, :]).then_inc(E.sy.h, 16)
                    E.sy.n += 16
                cs_ok = E.sy.n

                @block.tensor
                def _(t, t0=t0, conv_ac=conv_ac):
                    t.wait_ge(E.ac.h, conv_ac)
                    t.wait_ge(E.ve.h, E.ve.n)
                    for grp in range(2):
                        for k in range(4):
                            ch = grp * 4 + k
                            inst = t.transpose(pt[grp][:, k * 128:(k + 1) * 128],
                                               QP(ch)[:, 2 + t0:2 + t0 + 128], ident[:])
                        E.pe.inc(inst)

                @block.vector
                def _(v):
                    v.wait_ge(E.pe.h, E.pe.n)
                    v.tensor_copy(tmb[:, 0:512], pt[0][:])
                    fence(v, E.ve, v.tensor_copy(tmb[:, 512:1024], pt[1][:]))

                @block.vector
                def _(v, tt=tt):
                    fence(v, E.ve, v.tensor_scalar(KH, kmean[:, tt * 64:(tt + 1) * 64], 0.5, None, ALU.mult))
                    for h in range(16):
                        inst = v.tensor_tensor(QW[:, h * 64:(h + 1) * 64],
                                               tmb[:, h * 64:(h + 1) * 64], KH, ALU.add)
                    fence(v, E.ve, inst)

                @block.scalar
                def _(s):
                    s.wait_ge(E.ve.h, E.ve.n)
                    inst = s.activation(SQ, QW, AF.Square)
                    E.ac.inc(inst)

                @block.vector
                def _(v):
                    v.wait_ge(E.ac.h, E.ac.n)
                    for h in range(16):
                        inst = v.tensor_reduce(ssb[:, h:h + 1], SQ[:, h * 64:(h + 1) * 64], AXX, ALU.add)
                    fence(v, E.ve, inst)
                    fence(v, E.ve, v.tensor_scalar(ssb[:, 0:16], ssb[:, 0:16], 1e-24, None, ALU.max))

                @block.scalar
                def _(s):
                    s.wait_ge(E.ve.h, E.ve.n)
                    fence(s, E.ac, s.activation(ssb[:, 16:32], ssb[:, 0:16], AF.Ln))
                    fence(s, E.ac, s.activation(ssb[:, 16:32], ssb[:, 16:32], AF.Exp,
                                                scale=-0.5, bias=cvals[:, 1:2]))

                @block.vector
                def _(v):
                    v.wait_ge(E.ac.h, E.ac.n)
                    for h in range(16):
                        inst = v.tensor_scalar(QW[:, h * 64:(h + 1) * 64], QW[:, h * 64:(h + 1) * 64],
                                               ssb[:, 16 + h:17 + h], None, ALU.mult)
                    fence(v, E.ve, inst)

                q3 = QW.rearrange("p (f t) -> p f t", t=2)
                o3 = qko[:, 0:1024].rearrange("p (f t) -> p f t", t=2)
                cosv = csb[:, 0:512].unsqueeze(2)
                sinv = csb[:, 512:1024].unsqueeze(2)

                @block.vector
                def _(v, cs_ok=cs_ok):
                    v.wait_ge(E.sy.h, cs_ok)
                    v.tensor_tensor(T1.unsqueeze(2), q3[:, :, 0:1], cosv, ALU.mult)
                    fence(v, E.ve, v.tensor_tensor(T2.unsqueeze(2), q3[:, :, 1:2], sinv, ALU.mult))
                    fence(v, E.ve, v.tensor_tensor(o3[:, :, 0:1], T1.unsqueeze(2), T2.unsqueeze(2), ALU.subtract))
                    v.tensor_tensor(T1.unsqueeze(2), q3[:, :, 0:1], sinv, ALU.mult)
                    fence(v, E.ve, v.tensor_tensor(T2.unsqueeze(2), q3[:, :, 1:2], cosv, ALU.mult))
                    fence(v, E.ve, v.tensor_tensor(o3[:, :, 1:2], T1.unsqueeze(2), T2.unsqueeze(2), ALU.add))

                if tt == 0 and "stop_a3t0" in E.dbg:
                    _dump_sbuf(E, [qko[:, 0:1024], csb[:], QW])
                    return

                for c, (hA, hB) in enumerate(PAIRS):
                    bank = pt[c % 2]

                    @block.tensor
                    def _(t, hA=hA, hB=hB, bank=bank):
                        t.wait_ge(E.ve.h, E.ve.n)
                        t.transpose(bank[0:64, 0:128], qko[:, hA * 64:(hA + 1) * 64], ident[:])
                        inst = t.transpose(bank[0:64, 128:256], qko[:, hB * 64:(hB + 1) * 64], ident[:])
                        E.pe.inc(inst)

                    @block.vector
                    def _(v, c=c, bank=bank, t0=t0):
                        v.wait_ge(E.pe.h, E.pe.n)
                        v.tensor_copy(QT[0:64, c * 2048 + t0:c * 2048 + t0 + 128], bank[0:64, 0:128])
                        inst = v.tensor_copy(QT[64:128, c * 2048 + t0:c * 2048 + t0 + 128], bank[0:64, 128:256])
                        fence(v, E.ve, inst)

                @block.tensor
                def _(t, t0=t0):
                    t.wait_ge(E.ve.h, E.ve.n)
                    t.transpose(pt[0][:, 0:128], KP(0)[:, 2 + t0:2 + t0 + 128], ident[:])
                    inst = t.transpose(pt[0][:, 128:256], KP(1)[:, 2 + t0:2 + t0 + 128], ident[:])
                    E.pe.inc(inst)

                @block.vector
                def _(v):
                    v.wait_ge(E.pe.h, E.pe.n)
                    fence(v, E.ve, v.tensor_copy(tmb[:, 0:256], pt[0][:, 0:256]))

                @block.vector
                def _(v, tt=tt):
                    fence(v, E.ve, v.tensor_scalar(QH, qmean[:, tt * 64:(tt + 1) * 64], 0.5, None, ALU.mult))
                    for h in range(4):
                        inst = v.tensor_tensor(QW[:, h * 64:(h + 1) * 64],
                                               tmb[:, h * 64:(h + 1) * 64], QH, ALU.add)
                    fence(v, E.ve, inst)

                @block.scalar
                def _(s):
                    s.wait_ge(E.ve.h, E.ve.n)
                    inst = s.activation(SQ[:, 0:256], QW[:, 0:256], AF.Square)
                    E.ac.inc(inst)

                @block.vector
                def _(v):
                    v.wait_ge(E.ac.h, E.ac.n)
                    for h in range(4):
                        inst = v.tensor_reduce(ssb[:, 32 + h:33 + h], SQ[:, h * 64:(h + 1) * 64], AXX, ALU.add)
                    fence(v, E.ve, inst)
                    fence(v, E.ve, v.tensor_scalar(ssb[:, 32:36], ssb[:, 32:36], 1e-24, None, ALU.max))

                @block.scalar
                def _(s):
                    s.wait_ge(E.ve.h, E.ve.n)
                    fence(s, E.ac, s.activation(ssb[:, 36:40], ssb[:, 32:36], AF.Ln))
                    fence(s, E.ac, s.activation(ssb[:, 36:40], ssb[:, 36:40], AF.Exp, scale=-0.5))

                @block.vector
                def _(v):
                    v.wait_ge(E.ac.h, E.ac.n)
                    fence(v, E.ve, v.tensor_scalar(ssb[:, 36:40], ssb[:, 36:40], ktb[:, 0:1], None, ALU.mult))
                    for h in range(4):
                        inst = v.tensor_scalar(QW[:, h * 64:(h + 1) * 64], QW[:, h * 64:(h + 1) * 64],
                                               ssb[:, 36 + h:37 + h], None, ALU.mult)
                    fence(v, E.ve, inst)

                k3 = QW[:, 0:256].rearrange("p (f t) -> p f t", t=2)
                ko3 = qko[:, 1024:1280].rearrange("p (f t) -> p f t", t=2)
                cosk = csb[:, 0:128].unsqueeze(2)
                sink = csb[:, 512:640].unsqueeze(2)

                @block.vector
                def _(v, cs_ok=cs_ok):
                    v.wait_ge(E.sy.h, cs_ok)
                    v.tensor_tensor(T1[:, 0:128].unsqueeze(2), k3[:, :, 0:1], cosk, ALU.mult)
                    fence(v, E.ve, v.tensor_tensor(T2[:, 0:128].unsqueeze(2), k3[:, :, 1:2], sink, ALU.mult))
                    fence(v, E.ve, v.tensor_tensor(ko3[:, :, 0:1], T1[:, 0:128].unsqueeze(2),
                                                   T2[:, 0:128].unsqueeze(2), ALU.subtract))
                    v.tensor_tensor(T1[:, 0:128].unsqueeze(2), k3[:, :, 0:1], sink, ALU.mult)
                    fence(v, E.ve, v.tensor_tensor(T2[:, 0:128].unsqueeze(2), k3[:, :, 1:2], cosk, ALU.mult))
                    fence(v, E.ve, v.tensor_tensor(ko3[:, :, 1:2], T1[:, 0:128].unsqueeze(2),
                                                   T2[:, 0:128].unsqueeze(2), ALU.add))

                @block.tensor
                def _(t):
                    t.wait_ge(E.ve.h, E.ve.n)
                    for g in range(4):
                        inst = t.transpose(pt[1][0:64, g * 128:(g + 1) * 128],
                                           qko[:, 1024 + g * 64:1024 + (g + 1) * 64], ident[:])
                    E.pe.inc(inst)

                @block.vector
                def _(v, t0=t0):
                    v.wait_ge(E.pe.h, E.pe.n)
                    for g in range(4):
                        inst = v.tensor_copy(KT[(g % 2) * 64:(g % 2) * 64 + 64,
                                                (g // 2) * 2048 + t0:(g // 2) * 2048 + t0 + 128],
                                             pt[1][0:64, g * 128:(g + 1) * 128])
                    fence(v, E.ve, inst)

                @block.tensor
                def _(t, t0=t0):
                    t.wait_ge(E.ve.h, E.ve.n)
                    for b in range(2):
                        inst = t.transpose(pt[0][:, b * 128:(b + 1) * 128],
                                           vfm[:, b * 2048 + t0:b * 2048 + t0 + 128], ident[:])
                    E.pe.inc(inst)

                @block.vector
                def _(v, tt=tt):
                    v.wait_ge(E.pe.h, E.pe.n)
                    fence(v, E.ve, v.tensor_copy(VT[:, tt * 256:(tt + 1) * 256], pt[0][:, 0:256]))

        if "stop_a3" in E.dbg:
            _dump_sbuf(E, [QT[:], KT[:], VT[:]])
            return

        fm_cm = nc.sbuf_tensor("a4_fm", [128, 8 * 2048], BF16)
        attn_fm = fm_cm.__enter__()
        _emit_a4(E, QT, KT, VT, attn_fm)
        for cm in reversed(qt_cms):
            cm.__exit__(None, None, None)
        if "stop_a4" in E.dbg:
            _dump_sbuf(E, [attn_fm[:]])
            fm_cm.__exit__(None, None, None)
            return
        _emit_a5(E, attn_fm, stat)
        fm_cm.__exit__(None, None, None)


def _emit_a4(E, QT, KT, VT, attn_fm):
    nc, block, T, C = E.nc, E.block, E.T, E.consts
    pb, pt = E.pb, E.pt
    ident, caus = C["ident"], C["caus"]

    if True:
        with (nc.sbuf_tensor("a4_sr", [128, 2 * 2048], F32) as srow,
              nc.sbuf_tensor("a4_pr", [128, 2 * 2048], BF16) as prb,
              nc.sbuf_tensor("a4_pt", [128, 2 * 16 * 512], BF16) as ptb,
              nc.sbuf_tensor("a4_sm", [128, 8], F32) as smb):
            SR = lambda m: srow[:, m * 2048:(m + 1) * 2048]
            PR = lambda m: prb[:, m * 2048:(m + 1) * 2048]
            PT = lambda m: ptb[:, m * 16 * 512:(m + 1) * 16 * 512]

            scev = []
            for qg in range(4):
                nkc = 4 * qg + 4
                for c, (hA, hB) in enumerate(PAIRS):
                    kvc = c // 4
                    for sq in range(4):
                        qt = qg * 4 + sq
                        kmax = (qt + 1) * 128
                        nk512 = (kmax + 511) // 512
                        for kc in range(nk512):
                            w = min(512, kmax - kc * 512)
                            bA = pb[2 * (kc % 2)]
                            bB = pb[2 * (kc % 2) + 1]

                            @block.tensor
                            def _(t, c=c, qt=qt, kc=kc, w=w, bA=bA, bB=bB, kvc=kvc,
                                  gac=(scev[-2] if len(scev) >= 2 else 0)):
                                if gac:
                                    t.wait_ge(E.ac.h, gac)
                                t.matmul(bA[:, 0:w], QT[0:64, c * 2048 + qt * 128:c * 2048 + (qt + 1) * 128],
                                         KT[0:64, kvc * 2048 + kc * 512:kvc * 2048 + kc * 512 + w],
                                         start=True, stop=True)
                                inst = t.matmul(bB[:, 0:w],
                                                QT[64:128, c * 2048 + qt * 128:c * 2048 + (qt + 1) * 128],
                                                KT[64:128, kvc * 2048 + kc * 512:kvc * 2048 + kc * 512 + w],
                                                start=True, stop=True, tile_position=(64, 0))
                                E.pe.inc(inst)

                            @block.scalar
                            def _(s, kc=kc, w=w, bA=bA, bB=bB):
                                s.wait_ge(E.pe.h, E.pe.n)
                                s.activation(SR(0)[:, kc * 512:kc * 512 + w], bA[:, 0:w], AF.Copy)
                                inst = s.activation(SR(1)[:, kc * 512:kc * 512 + w], bB[:, 0:w], AF.Copy)
                                E.ac.inc(inst)
                            scev.append(E.ac.n)

                        for m in range(2):
                            @block.vector
                            def _(v, m=m, qt=qt, kmax=kmax):
                                v.wait_ge(E.ac.h, E.ac.n)
                                v.wait_ge(E.pe.h, E.pe.n)
                                fence(v, E.ve, v.tensor_tensor(SR(m)[:, qt * 128:kmax],
                                                               SR(m)[:, qt * 128:kmax], caus[:], ALU.add))
                                fence(v, E.ve, v.tensor_reduce(smb[:, m:m + 1], SR(m)[:, 0:kmax],
                                                               AXX, ALU.max, negate=True))

                            @block.scalar
                            def _(s, m=m, kmax=kmax):
                                s.wait_ge(E.ve.h, E.ve.n)
                                inst = s.activation(PR(m)[:, 0:kmax], SR(m)[:, 0:kmax], AF.Exp,
                                                    bias=smb[:, m:m + 1], accum_out=smb[:, 2 + m:3 + m])
                                E.ac.inc(inst)

                            @block.vector
                            def _(v, m=m, kmax=kmax):
                                v.wait_ge(E.ac.h, E.ac.n)
                                fence(v, E.ve, v.reciprocal(smb[:, 4 + m:5 + m], smb[:, 2 + m:3 + m]))
                                inst = v.tensor_scalar(PR(m)[:, 0:kmax], PR(m)[:, 0:kmax],
                                                       smb[:, 4 + m:5 + m], None, ALU.mult)
                                fence(v, E.ve, inst)

                            for kcg in range((qt + 1 + 3) // 4):
                                bank = pt[kcg % 2]
                                nk = min(4, qt + 1 - kcg * 4)

                                @block.tensor
                                def _(t, m=m, kcg=kcg, nk=nk, bank=bank):
                                    t.wait_ge(E.ve.h, E.ve.n)
                                    for j in range(nk):
                                        inst = t.transpose(bank[:, j * 128:(j + 1) * 128],
                                                           PR(m)[:, (kcg * 4 + j) * 128:(kcg * 4 + j + 1) * 128],
                                                           ident[:])
                                    E.pe.inc(inst)

                                @block.vector
                                def _(v, m=m, sq=sq, kcg=kcg, nk=nk, bank=bank):
                                    v.wait_ge(E.pe.h, E.pe.n)
                                    dst = PT(m).rearrange("p (kc j) -> p kc j", j=512)
                                    inst = v.tensor_copy(
                                        dst[:, kcg * 4:kcg * 4 + nk, sq * 128:(sq + 1) * 128],
                                        bank[:, 0:nk * 128].rearrange("p (kc j) -> p kc j", j=128))
                                    fence(v, E.ve, inst)

                    for m in range(2):
                        @block.vector
                        def _(v, m=m, qg=qg, nkc=nkc):
                            inst = None
                            for sq in range(4):
                                qt = qg * 4 + sq
                                for kc in range(qt + 1, nkc):
                                    inst = v.memset(PT(m)[:, kc * 512 + sq * 128:kc * 512 + (sq + 1) * 128], 0.0)
                            if inst is not None:
                                fence(v, E.ve, inst)

                    @block.tensor
                    def _(t, hA=hA, hB=hB, nkc=nkc):
                        t.wait_ge(E.ve.h, E.ve.n)
                        t.wait_ge(E.ac.h, E.ac.n)
                        gA, gB = hA // 4, hB // 4
                        for kc in range(nkc):
                            vA = VT[:, kc * 256 + (gA // 2) * 128 + (gA % 2) * 64:
                                    kc * 256 + (gA // 2) * 128 + (gA % 2) * 64 + 64]
                            vB = VT[:, kc * 256 + (gB // 2) * 128 + (gB % 2) * 64:
                                    kc * 256 + (gB // 2) * 128 + (gB % 2) * 64 + 64]
                            t.matmul(pb[4][0:64, :], vA, PT(0)[:, kc * 512:(kc + 1) * 512],
                                     start=(kc == 0), stop=(kc == nkc - 1))
                            inst = t.matmul(pb[4][64:128, :], vB, PT(1)[:, kc * 512:(kc + 1) * 512],
                                            start=(kc == 0), stop=(kc == nkc - 1),
                                            tile_position=(0, 64))
                        E.pe.inc(inst)

                    @block.scalar
                    def _(s, c=c, qg=qg):
                        s.wait_ge(E.pe.h, E.pe.n)
                        inst = s.activation(attn_fm[:, c * 2048 + qg * 512:c * 2048 + (qg + 1) * 512],
                                            pb[4][:], AF.Copy)
                        E.ac.inc(inst)



def _emit_a5(E, attn_fm, stat):
    nc, block, T, C = E.nc, E.block, E.T, E.consts
    pb, pt = E.pb, E.pt
    ident = C["ident"]
    _scope_barrier(E)
    if True:
        with (nc.sbuf_tensor("a5_wo", [128, 8 * 4096], BF16) as wob,
              nc.sbuf_tensor("a5_y", [128, DM], F32) as yb,
              nc.sbuf_tensor("a5_x", [128, DM], F32) as xb,
              nc.sbuf_tensor("a5_n", [128, DM], BF16) as nb2,
              nc.sbuf_tensor("a5_t", [128, DM], BF16) as tb2):
            @block.gpsimd
            def _(g):
                for ch in range(8):
                    g.dma_start(out=wob[:, ch * 4096:(ch + 1) * 4096],
                                in_=T["woTp"][ch * 128:(ch + 1) * 128, :]).then_inc(E.gp.h, 16)
                    E.gp.n += 16
            wo_ok = E.gp.n

            for tt in range(NT):
                t0 = tt * 128

                @block.sync
                def _(sy, tt=tt):
                    sy.wait_ge(E.ve.h, E.ve.n)
                    sy.wait_ge(E.gp.h, E.gp.n)
                    sy.dma_start(out=xb[:], in_=T["x_in"][tt * 128:(tt + 1) * 128, :]).then_inc(E.sy.h, 16)
                    E.sy.n += 16
                x_ok = E.sy.n

                woev = []
                for mg in range(8):
                    bank = pb[mg % 4]

                    @block.tensor
                    def _(t, mg=mg, t0=t0, bank=bank, wo_ok=wo_ok,
                          gac=(woev[-4] if len(woev) >= 4 else 0)):
                        t.wait_ge(E.gp.h, wo_ok)
                        t.wait_ge(E.ac.h, gac if gac else E.ac.n)
                        for ch in range(8):
                            inst = t.matmul(bank[:], attn_fm[:, ch * 2048 + t0:ch * 2048 + t0 + 128],
                                            wob[:, ch * 4096 + mg * 512:ch * 4096 + (mg + 1) * 512],
                                            start=(ch == 0), stop=(ch == 7))
                        E.pe.inc(inst)

                    @block.scalar
                    def _(s, mg=mg, bank=bank):
                        s.wait_ge(E.pe.h, E.pe.n)
                        inst = s.activation(yb[:, mg * 512:(mg + 1) * 512], bank[:], AF.Copy)
                        E.ac.inc(inst)
                    woev.append(E.ac.n)

                @block.vector
                def _(v):
                    v.wait_ge(E.ac.h, E.ac.n)
                _ln_core(E, yb, nb2, stat, DM)

                @block.vector
                def _(v, x_ok=x_ok):
                    v.wait_ge(E.ac.h, E.ac.n)
                    v.wait_ge(E.sy.h, x_ok)
                    fence(v, E.ve, v.tensor_scalar(yb[:], yb[:], stat[:, 3:4], None, ALU.mult))
                    fence(v, E.ve, v.tensor_tensor(xb[:], xb[:], yb[:], ALU.add))

                @block.gpsimd
                def _(g, tt=tt):
                    g.wait_ge(E.ve.h, E.ve.n)
                    g.dma_start(out=T["x1_d"][tt * 128:(tt + 1) * 128, :], in_=xb[:]).then_inc(E.gp.h, 16)
                    E.gp.n += 16

                _ln_core(E, xb, nb2, stat, DM)

                @block.vector
                def _(v):
                    v.wait_ge(E.ac.h, E.ac.n)
                    fence(v, E.ve, v.tensor_scalar(nb2[:], xb[:], stat[:, 3:4], None, ALU.mult))

                for grp in range(8):
                    bank = pt[grp % 2]

                    @block.tensor
                    def _(t, grp=grp, bank=bank):
                        t.wait_ge(E.ve.h, E.ve.n)
                        for k in range(4):
                            ch = grp * 4 + k
                            inst = t.transpose(bank[:, k * 128:(k + 1) * 128],
                                               nb2[:, ch * 128:(ch + 1) * 128], ident[:])
                        E.pe.inc(inst)

                    @block.vector
                    def _(v, grp=grp, bank=bank):
                        v.wait_ge(E.pe.h, E.pe.n)
                        fence(v, E.ve, v.tensor_copy(tb2[:, grp * 512:(grp + 1) * 512], bank[:]))

                @block.sync
                def _(sy, tt=tt):
                    sy.wait_ge(E.ve.h, E.ve.n)
                    for ch in range(32):
                        sy.dma_start(out=T["xn2T_d"][ch * 128:(ch + 1) * 128, tt * 128:(tt + 1) * 128],
                                     in_=tb2[:, ch * 128:(ch + 1) * 128]).then_inc(E.sy.h, 16)
                        E.sy.n += 16
    E.mark["xn2T_gp"] = 0
    E.mark["xn2T_sy"] = E.sy.n
    E.mark["x1_gp"] = E.gp.n


def _emit_f1(E):
    nc, block, T = E.nc, E.block, E.T
    pb = E.pb
    lora_a, lorab = E.consts["lora_a"], E.consts["lorab"]
    G = 1024
    NGRP = S // G
    NJT = DFFP // 512
    HALF = 16 * 512

    with (nc.sbuf_tensor("f1_xn", [128, 32 * G], BF16) as xnb,
          nc.sbuf_tensor("f1_wg", [128, 3 * HALF], BF16) as wgb,
          nc.sbuf_tensor("f1_wu", [128, 3 * HALF], BF16) as wub,
          nc.sbuf_tensor("f1_u", [128, G], BF16) as ubuf,
          nc.sbuf_tensor("f1_rl", [128, 2 * 512], F32) as rlbuf,
          nc.sbuf_tensor("f1_x3", [128, 4 * 512], BF16) as x3t):

        def wslice(buf, jt, ch, js, jw):
            kh, kc = ch // 16, ch % 16
            slot = (2 * jt + kh) % 3
            base = slot * HALF + (kc // 8) * 8 * jw + (kc % 8) * jw + js * 128
            return buf[:, base:base + 128]

        def emit_wload_half(sy, hidx, pe_gate):
            jt, kh = hidx // 2, hidx % 2
            jw = 512
            j0 = jt * 512
            slot = hidx % 3
            if pe_gate:
                sy.wait_ge(E.pe.h, pe_gate)
            for half in range(2):
                r0 = kh * 2048 + half * 1024
                for wf, buf in ((T["wg_f"], wgb), (T["wu_f"], wub)):
                    src = wf[r0:r0 + 1024, j0:j0 + jw].rearrange("(c p) j -> p c j", p=128)
                    sy.dma_start(out=buf[:, slot * HALF + half * 8 * jw:
                                         slot * HALF + (half * 8 + 8) * jw],
                                 in_=src).then_inc(E.sy.h, 16)
                    E.sy.n += 16
            return E.sy.n

        _scope_barrier(E)
        last_pe_prev_grp = E.pe.n
        ev_hist = []
        st_prev = 0
        for grp in range(NGRP):
            t0 = grp * G

            @block.scalar
            def _(s, t0=t0, gate=last_pe_prev_grp):
                s.wait_ge(E.gp.h, E.mark.get("xn2T_gp", 0))
                if E.mark.get("xn2T_sy", 0):
                    s.wait_ge(E.sy.h, E.mark["xn2T_sy"])
                if gate:
                    s.wait_ge(E.pe.h, gate)
                for ch in range(32):
                    s.dma_start(out=xnb[:, ch * G:(ch + 1) * G],
                                in_=T["xn2T_d"][ch * 128:(ch + 1) * 128, t0:t0 + G]).then_inc(E.sc.h, 16)
                    E.sc.n += 16
            xn_ready = E.sc.n

            for ts in range(G // 512):
                tc0 = ts * 512

                @block.tensor
                def _(t, tc0=tc0, xn_ready=xn_ready, ts=ts, grp=grp, gate=last_pe_prev_grp):
                    t.wait_ge(E.sc.h, xn_ready)
                    t.wait_ge(E.gp.h, max(E.mark["lora_gp"], E.mark.get("lorab_gp", 0)))
                    if ts > 0 or grp > 0:
                        t.wait_ge(E.ve.h, E.ve.n)
                    for ch in range(32):
                        rx = xnb[:, ch * G + tc0: ch * G + tc0 + 512]
                        t.matmul(pb[4][0:16, :], lora_a[:, ch * R:(ch + 1) * R], rx,
                                 start=(ch == 0), stop=(ch == 31))
                    for ch in range(32):
                        rx = xnb[:, ch * G + tc0: ch * G + tc0 + 512]
                        inst = t.matmul(pb[5][0:16, :], lora_a[:, (32 + ch) * R:(33 + ch) * R], rx,
                                        start=(ch == 0), stop=(ch == 31))
                    E.pe.inc(inst)

                @block.vector
                def _(v, tc0=tc0):
                    v.wait_ge(E.pe.h, E.pe.n)
                    v.tensor_copy(ubuf[0:16, tc0:tc0 + 512], pb[4][0:16, :])
                    fence(v, E.ve, v.tensor_copy(ubuf[32:48, tc0:tc0 + 512], pb[5][0:16, :]))
            u_ready = E.ve.n

            half_done = {}

            @block.sync
            def _(sy, gate=last_pe_prev_grp):
                sy.wait_ge(E.cc.h, E.mark.get('wbarrier', E.cc.n))
                for h in (0, 1, 2):
                    emit_wload_half(sy, h, gate)
                    half_done[h] = E.sy.n
            next_half = 3

            it = 0
            sg = 0
            for jt in range(NJT):
                j0 = jt * 512
                jw = 512
                njs = 4
                for ts in range(G // 512):
                    tc0 = ts * 512
                    for js in range(njs):
                        jj = j0 + js * 128
                        par = it % 2
                        pyA, pyB = pb[2 * par], pb[2 * par + 1]
                        ev_gate = ev_hist[-2] if len(ev_hist) >= 2 else 0

                        @block.tensor
                        def _(t, jt=jt, jw=jw, js=js, tc0=tc0, jj=jj,
                              pyA=pyA, pyB=pyB, ev_gate=ev_gate,
                              xn_ready=xn_ready, u_ready=u_ready,
                              wr=None):
                            t.wait_ge(E.sc.h, xn_ready)
                            t.wait_ge(E.sy.h, half_done[2 * jt + 1])
                            t.wait_ge(E.ve.h, max(ev_gate, u_ready))
                            for mat, py, buf in ((0, pyA, wgb), (1, pyB, wub)):
                                for ch in range(32):
                                    lw = wslice(buf, jt, ch, js, jw)
                                    rx = xnb[:, ch * G + tc0: ch * G + tc0 + 512]
                                    t.matmul(py[:], lw, rx, start=(ch == 0), stop=False)
                                lb = lorab[mat * 32:mat * 32 + 16, jj:jj + 128]
                                uu = ubuf[mat * 32:mat * 32 + 16, tc0:tc0 + 512]
                                inst = t.matmul(py[:], lb, uu, start=False, stop=True)
                            E.pe.inc(inst)
                        pe_now = E.pe.n

                        @block.scalar
                        def _(s, pyA=pyA, pe_now=pe_now, par=par,
                              gve=(ev_hist[-2] if len(ev_hist) >= 2 else 0)):
                            s.wait_ge(E.pe.h, pe_now)
                            if gve:
                                s.wait_ge(E.ve.h, gve)
                            inst = s.activation(rlbuf[:, par * 512:(par + 1) * 512], pyA[:], AF.Relu)
                            E.ac.inc(inst)

                        @block.vector
                        def _(v, pyB=pyB, js=js, par=par, st_prev=st_prev):
                            v.wait_ge(E.ac.h, E.ac.n)
                            if st_prev:
                                v.wait_ge(E.gp.h, st_prev)
                            inst = v.tensor_tensor(x3t[:, js * 512:(js + 1) * 512],
                                                   rlbuf[:, par * 512:(par + 1) * 512],
                                                   pyB[:], ALU.mult)
                            fence(v, E.ve, inst)
                        ev_hist.append(E.ve.n)
                        it += 1

                    @block.gpsimd
                    def _(g, j0=j0, jw=jw, njs=njs, t0=t0, tc0=tc0):
                        g.wait_ge(E.ve.h, E.ve.n)
                        src = x3t[:, 0:njs * 512].rearrange("p (c j) -> p c j", j=512)
                        g.dma_start(out=T["x3T_d"][j0:j0 + jw, t0 + tc0:t0 + tc0 + 512]
                                    .rearrange("(c p) j -> p c j", p=128),
                                    in_=src).then_inc(E.gp.h, 16)
                        E.gp.n += 16
                    st_prev = E.gp.n
                    sg += 1

                pe_after = E.pe.n
                loads = []
                while next_half < 2 * NJT and next_half <= 2 * jt + 4:
                    loads.append(next_half)
                    next_half += 1
                if loads:
                    @block.sync
                    def _(sy, loads=tuple(loads), gate=pe_after):
                        for h in loads:
                            emit_wload_half(sy, h, gate)
                            half_done[h] = E.sy.n
            last_pe_prev_grp = E.pe.n
    E.mark["x3T_gp"] = E.gp.n


def _emit_f2(E):
    nc, block, T = E.nc, E.block, E.T
    pb = E.pb
    lora_a, lorab = E.consts["lora_a"], E.consts["lorab"]
    NTG = 4
    NCH = DFF // 128
    RANK_NCH = [r // 128 for r in D_SHARD_ROWS]
    RANK_C0 = [sum(RANK_NCH[:r]) for r in range(8)]
    SLOTW = 11 * 512

    with (nc.sbuf_tensor("f2_x3", [128, NCH * 512], BF16) as x3b,
          nc.sbuf_tensor("f2_wd", [128, 3 * SLOTW], BF16) as wdb,
          nc.sbuf_tensor("f2_ud", [128, 512], BF16) as udb,
          nc.sbuf_tensor("f2_y3", [128, 4 * 512], F32) as y3t):

        _scope_barrier(E)
        for tg in range(NTG):
            t0 = tg * 512
            gate_pe = E.pe.n

            @block.scalar
            def _(s, t0=t0, gate_pe=gate_pe):
                s.wait_ge(E.gp.h, E.mark["x3T_gp"])
                if gate_pe:
                    s.wait_ge(E.pe.h, gate_pe)
                for ch in range(NCH):
                    s.dma_start(out=x3b[:, ch * 512:(ch + 1) * 512],
                                in_=T["x3T_d"][ch * 128:(ch + 1) * 128, t0:t0 + 512]).then_inc(E.sc.h, 16)
                    E.sc.n += 16
            x3_ready = E.sc.n

            wd_ready = {}
            pe_rank = {}

            def emit_rank_load(sy, m, r, gate):
                if gate:
                    sy.wait_ge(E.pe.h, gate)
                ri = m * 8 + r
                slot = ri % 3
                g0 = r * D_PAD
                nrow = D_SHARD_ROWS[r]
                src = T["wd_f"][g0:g0 + nrow, m * 512:m * 512 + 512]\
                    .rearrange("(c p) j -> p c j", p=128)
                sy.dma_start(out=wdb[:, slot * SLOTW:slot * SLOTW + (nrow // 128) * 512],
                             in_=src).then_inc(E.sy.h, 16)
                E.sy.n += 16
                wd_ready[ri] = E.sy.n

            @block.sync
            def _(sy, tg=tg, gate_pe=gate_pe):
                sy.wait_ge(E.cc.h, E.mark.get('wbarrier', E.cc.n))
                for ri in range(3):
                    emit_rank_load(sy, ri // 8, ri % 8, gate_pe)

            ud_gate = 0
            ev_prev_m = (E.ve.n, E.ac.n)
            for m in range(8):
                m0 = m * 512
                for r in range(8):
                    ri = m * 8 + r
                    slot = ri % 3
                    is_first = (ri == 0)
                    is_last_rank = (r == 7)

                    @block.tensor
                    def _(t, m=m, r=r, ri=ri, slot=slot, x3_ready=x3_ready,
                          ev_prev_m=ev_prev_m):
                        t.wait_ge(E.sc.h, x3_ready)
                        t.wait_ge(E.sy.h, wd_ready[ri])
                        if r == 0:
                            if ev_prev_m[0]:
                                t.wait_ge(E.ve.h, ev_prev_m[0])
                            if ev_prev_m[1]:
                                t.wait_ge(E.ac.h, ev_prev_m[1])
                        nch_r = RANK_NCH[r]
                        for k in range(nch_r):
                            ch = RANK_C0[r] + k
                            rw = wdb[:, slot * SLOTW + k * 512:slot * SLOTW + (k + 1) * 512]
                            for tsub in range(4):
                                lx = x3b[:, ch * 512 + tsub * 128:ch * 512 + tsub * 128 + 128]
                                inst = t.matmul(pb[tsub][:], lx, rw,
                                                start=(ch == 0), stop=False)
                            if m == 0:
                                inst = t.matmul(pb[4][0:16, :],
                                                lora_a[:, (64 + ch) * R:(65 + ch) * R],
                                                x3b[:, ch * 512:(ch + 1) * 512],
                                                start=(ch == 0), stop=(ch == NCH - 1))
                        E.pe.inc(inst)
                    pe_rank[ri] = E.pe.n

                    if ri + 3 < 64:
                        @block.sync
                        def _(sy, m2=(ri + 3) // 8, r2=(ri + 3) % 8, gate=pe_rank[ri]):
                            emit_rank_load(sy, m2, r2, gate)

                if m == 0:
                    @block.vector
                    def _(v):
                        v.wait_ge(E.pe.h, E.pe.n)
                        fence(v, E.ve, v.tensor_copy(udb[64:80, :], pb[4][0:16, :]))
                    ud_gate = E.ve.n

                @block.tensor
                def _(t, m0=m0, ud_gate=ud_gate):
                    t.wait_ge(E.ve.h, ud_gate)
                    for tsub in range(4):
                        inst = t.matmul(pb[tsub][:], udb[64:80, tsub * 128:(tsub + 1) * 128],
                                        lorab[64:80, m0:m0 + 512], start=False, stop=True)
                    E.pe.inc(inst)
                pe_m = E.pe.n

                @block.scalar
                def _(s, pe_m=pe_m):
                    s.wait_ge(E.pe.h, pe_m)
                    s.wait_ge(E.gp.h, E.gp.n)
                    for tsub in range(4):
                        inst = s.activation(y3t[:, tsub * 512:(tsub + 1) * 512],
                                            pb[tsub][:], AF.Copy)
                    E.ac.inc(inst)
                @block.gpsimd
                def _(g, t0=t0, m0=m0):
                    g.wait_ge(E.ac.h, E.ac.n)
                    src = y3t[:].rearrange("p (c j) -> p c j", j=512)
                    g.dma_start(out=T["y3_d"][t0:t0 + 512, m0:m0 + 512]
                                .rearrange("(c p) j -> p c j", p=128),
                                in_=src).then_inc(E.gp.h, 16)
                    E.gp.n += 16
                ev_prev_m = (0, E.ac.n)
            E.mark[f"y3_tg{tg}_gp"] = E.gp.n
    E.mark["y3_gp"] = E.gp.n


def _emit_f3(E):
    nc, block, T = E.nc, E.block, E.T
    with (nc.sbuf_tensor("f3_y", [128, DM], F32) as yb,
          nc.sbuf_tensor("f3_x1", [128, DM], F32) as x1b,
          nc.sbuf_tensor("f3_scr", [128, DM], F32) as scr,
          nc.sbuf_tensor("f3_st", [128, 4], F32) as stat,
          nc.sbuf_tensor("f3_o", [128, DM], F32) as ob):
        for tt in range(NT):
            tg = tt // 4

            @block.sync
            def _(sy, tt=tt, tg=tg):
                sy.wait_ge(E.gp.h, max(E.mark[f"y3_tg{tg}_gp"], E.mark.get("x1_gp", 0)))
                sy.wait_ge(E.ve.h, E.ve.n)
                sy.dma_start(out=yb[:], in_=T["y3_d"][tt * 128:(tt + 1) * 128, :]).then_inc(E.sy.h, 16)
                sy.dma_start(out=x1b[:], in_=T["x1_d"][tt * 128:(tt + 1) * 128, :]).then_inc(E.sy.h, 16)
                E.sy.n += 32

            @block.vector
            def _(v):
                v.wait_ge(E.sy.h, E.sy.n)
            _ln_core(E, yb, scr, stat, DM)

            @block.vector
            def _(v):
                v.wait_ge(E.ac.h, E.ac.n)
                v.wait_ge(E.st.h, E.st.n)
                fence(v, E.ve, v.tensor_scalar(yb[:], yb[:], stat[:, 3:4], None, ALU.mult))
                fence(v, E.ve, v.tensor_tensor(ob[:], yb[:], x1b[:], ALU.add))

            @block.gpsimd
            def _(g, tt=tt):
                g.wait_ge(E.ve.h, E.ve.n)
                g.dma_start(out=T["out_t"][tt * 128:(tt + 1) * 128, :], in_=ob[:]).then_inc(E.st.h, 16)
                E.st.n += 16


def kernel(**inputs):
    maps = _prep(inputs)
    nc = build()
    res = run_bass_kernel_spmd(nc, maps, core_ids=list(range(8)))
    return np.stack([res.results[i]["out"] for i in range(8)], axis=0)


# version 05
# speedup vs baseline: 1.0053x; 1.0053x over previous
import numpy as np
import ml_dtypes
import concourse.bass as bass
import concourse.mybir as mybir
from concourse.bass_utils import run_bass_kernel_spmd

F32 = mybir.dt.float32
BF16 = mybir.dt.bfloat16
I32 = mybir.dt.int32
I8 = mybir.dt.int8
AF = mybir.ActivationFunctionType
ALU = mybir.AluOpType
AXX = mybir.AxisListType.X

B, S, DM = 8, 2048, 4096
NH, NKV, HD = 16, 4, 64
LAT, KVD = NH * HD, NKV * HD
DFF, R = 11008, 16
DFFP = 11264
NT = S // 128
EPS = 1e-5

NF4 = np.array([-1.0, -0.6961928009986877, -0.5250730514526367, -0.39491748809814453,
                -0.28444138169288635, -0.18477343022823334, -0.09105003625154495, 0.0,
                0.07958029955625534, 0.16093020141124725, 0.2461123019456863,
                0.33791524171829224, 0.44070982933044434, 0.5626170039176941,
                0.7229568362236023, 1.0], dtype=np.float64)

PAIRS = [(0, 4), (1, 5), (2, 6), (3, 7), (8, 12), (9, 13), (10, 14), (11, 15)]

D_SHARD_ROWS = [1408] * 6 + [1280, 1280]
D_PAD = 1408
D_START = [0, 1408, 2816, 4224, 5632, 7040, 8448, 9728]


def _prep(inp):
    c = np.ascontiguousarray
    f32 = np.float32
    wqkvT = c(np.asarray(inp["L0_attn_w_qkv"]).T.astype(f32, copy=False))
    woT = np.asarray(inp["L0_attn_w_o"]).T
    perm = []
    for (a, b) in PAIRS:
        perm.extend(range(a * 64, a * 64 + 64))
        perm.extend(range(b * 64, b * 64 + 64))
    woTp = c(woT[np.array(perm), :].astype(f32, copy=False))
    qc1T = c(np.asarray(inp["L0_attn_qc1_w"]).reshape(16, 64, 64, 3).transpose(3, 0, 2, 1).astype(f32, copy=False))
    qc2T = c(np.asarray(inp["L0_attn_qc2_w"]).transpose(2, 1, 0).astype(f32, copy=False))
    kc1T = c(np.asarray(inp["L0_attn_kc1_w"]).reshape(4, 64, 64, 3).transpose(3, 0, 2, 1).astype(f32, copy=False))
    kc2T = c(np.asarray(inp["L0_attn_kc2_w"]).transpose(2, 1, 0).astype(f32, copy=False))
    qc1b = c(np.asarray(inp["L0_attn_qc1_b"]).reshape(8, 128).T.astype(f32, copy=False))
    qc2b = c(np.asarray(inp["L0_attn_qc2_b"]).reshape(8, 128).T.astype(f32, copy=False))
    kc1b = c(np.asarray(inp["L0_attn_kc1_b"]).reshape(2, 128).T.astype(f32, copy=False))
    kc2b = c(np.asarray(inp["L0_attn_kc2_b"]).reshape(2, 128).T.astype(f32, copy=False))
    ktemp = c(np.asarray(inp["L0_attn_key_temp"]).reshape(1, 1).astype(f32, copy=False))
    gT = np.asarray(inp["L0_ffn_g_codes"]).T
    gaT = np.asarray(inp["L0_ffn_g_absmax"]).T
    uT = np.asarray(inp["L0_ffn_u_codes"]).T
    uaT = np.asarray(inp["L0_ffn_u_absmax"]).T
    dT = np.asarray(inp["L0_ffn_d_codes"]).T
    daT = np.asarray(inp["L0_ffn_d_absmax"]).T
    x_all = np.asarray(inp["L0_attn_x"])

    def padc(a):
        o = np.zeros((a.shape[0], DFFP), np.int32)
        o[:, :DFF] = a
        return o

    def padf(a):
        o = np.zeros((a.shape[0], DFFP), f32)
        o[:, :DFF] = a.astype(f32, copy=False)
        return o

    glb_p = np.zeros((R, DFFP), f32); glb_p[:, :DFF] = np.asarray(inp["L0_ffn_g_lora_b"])
    ulb_p = np.zeros((R, DFFP), f32); ulb_p[:, :DFF] = np.asarray(inp["L0_ffn_u_lora_b"])

    maps = []
    for r in range(8):
        f0, nrow = D_START[r], D_SHARD_ROWS[r]
        dc = np.zeros((D_PAD, DM), np.int32)
        dc[:nrow] = dT[f0:f0 + nrow]
        da = np.zeros((D_PAD // 64, DM), f32)
        da[:nrow // 64] = daT[f0 // 64:(f0 + nrow) // 64]
        maps.append({
            "x": c(x_all[r].astype(f32, copy=False)),
            "wqkvT": wqkvT, "woTp": woTp,
            "qc1T": qc1T, "qc2T": qc2T, "kc1T": kc1T, "kc2T": kc2T,
            "qc1b": qc1b, "qc2b": qc2b, "kc1b": kc1b, "kc2b": kc2b,
            "ktemp": ktemp,
            "gla": c(np.asarray(inp["L0_ffn_g_lora_a"]).astype(f32, copy=False)),
            "glb": glb_p,
            "ula": c(np.asarray(inp["L0_ffn_u_lora_a"]).astype(f32, copy=False)),
            "ulb": ulb_p,
            "dla": c(np.asarray(inp["L0_ffn_d_lora_a"]).astype(f32, copy=False)),
            "dlb": c(np.asarray(inp["L0_ffn_d_lora_b"]).astype(f32, copy=False)),
            "gcodes": padc(gT[512 * r:512 * r + 512]),
            "gabs": padf(gaT[8 * r:8 * r + 8]),
            "ucodes": padc(uT[512 * r:512 * r + 512]),
            "uabs": padf(uaT[8 * r:8 * r + 8]),
            "dcodes": dc, "dabs": da,
        })
    return maps


class Ctr:
    def __init__(self, nc, name):
        self._cm = nc.semaphore(name)
        self.h = self._cm.__enter__()
        self.n = 0

    def inc(self, inst, k=1):
        inst.then_inc(self.h, k)
        self.n += k
        return self.n


def fence(eng, ctr, inst):
    ctr.inc(inst)
    eng.wait_ge(ctr.h, ctr.n)


class Env:
    def __init__(self, nc, block):
        self.nc = nc
        self.block = block
        self.sy = Ctr(nc, "c_sy")
        self.gp = Ctr(nc, "c_gp")
        self.gpf = Ctr(nc, "c_gpf")
        self.sc = Ctr(nc, "c_sc")
        self.ve = Ctr(nc, "c_ve")
        self.ac = Ctr(nc, "c_ac")
        self.pe = Ctr(nc, "c_pe")
        self.cc = Ctr(nc, "c_cc")
        self.st = Ctr(nc, "c_st")
        self.mark = {}


def _ln_core(E, xbuf, scr, stat, W):
    @E.block.vector
    def _(v):
        fence(v, E.ve, v.tensor_reduce(stat[:, 0:1], xbuf[:, 0:W], AXX, ALU.add))
        fence(v, E.ve, v.tensor_scalar(stat[:, 1:2], stat[:, 0:1], -1.0 / W, None, ALU.mult))
        fence(v, E.ve, v.tensor_scalar(xbuf[:, 0:W], xbuf[:, 0:W], stat[:, 1:2], None, ALU.add))

    @E.block.scalar
    def _(s):
        s.wait_ge(E.ve.h, E.ve.n)
        fence(s, E.ac, s.activation(scr[:, 0:W], xbuf[:, 0:W], AF.Square, accum_out=stat[:, 2:3]))

    @E.block.vector
    def _(v):
        v.wait_ge(E.ac.h, E.ac.n)
        fence(v, E.ve, v.tensor_scalar(stat[:, 2:3], stat[:, 2:3], 1.0 / W, EPS, ALU.mult, ALU.add))

    @E.block.scalar
    def _(s):
        s.wait_ge(E.ve.h, E.ve.n)
        fence(s, E.ac, s.activation(stat[:, 2:3], stat[:, 2:3], AF.Ln))
        fence(s, E.ac, s.activation(stat[:, 3:4], stat[:, 2:3], AF.Exp, scale=-0.5))


def build(dbg=(), attn=True, ffn=True):
    dbg = set(dbg)
    nc = bass.Bass("TRN2", target_bir_lowering=False, debug=False)
    dt, st = nc.dram_tensor, nc.sbuf_tensor

    T = {}
    T["x_in"] = dt("x", [S, DM], F32, kind="ExternalInput").ap()
    T["out_t"] = dt("out", [S, DM], F32, kind="ExternalOutput").ap()
    for nm, shape in [("wqkvT", [DM, 1536]), ("woTp", [LAT, DM]),
                      ("qc1T", [3, 16, 64, 64]), ("qc2T", [3, 1024, 1024]),
                      ("kc1T", [3, 4, 64, 64]), ("kc2T", [3, 256, 256]),
                      ("qc1b", [128, 8]), ("qc2b", [128, 8]),
                      ("kc1b", [128, 2]), ("kc2b", [128, 2]), ("ktemp", [1, 1]),
                      ("gla", [DM, R]), ("glb", [R, DFFP]), ("ula", [DM, R]),
                      ("ulb", [R, DFFP]), ("dla", [DFF, R]), ("dlb", [R, DM]),
                      ("gabs", [8, DFFP]), ("uabs", [8, DFFP]), ("dabs", [D_PAD // 64, DM])]:
        T[nm] = dt(nm, shape, F32, kind="ExternalInput").ap()
    for nm, shape in [("gcodes", [512, DFFP]), ("ucodes", [512, DFFP]), ("dcodes", [D_PAD, DM])]:
        T[nm] = dt(nm, shape, I32, kind="ExternalInput").ap()

    T["wg_f"] = dt("wg_f", [DM, DFFP], BF16, kind="Internal").ap()
    T["wu_f"] = dt("wu_f", [DM, DFFP], BF16, kind="Internal").ap()
    T["wd_f"] = dt("wd_f", [8 * D_PAD, DM], BF16, kind="Internal").ap()
    T["wg_sh"] = dt("wg_sh", [512, DFFP], BF16, kind="Internal").ap()
    T["wu_sh"] = dt("wu_sh", [512, DFFP], BF16, kind="Internal").ap()
    T["wd_sh"] = dt("wd_sh", [D_PAD, DM], BF16, kind="Internal").ap()
    T["x1_d"] = dt("x1_d", [S, DM], F32, kind="Internal" if attn else "ExternalInput").ap()
    T["xn2T_d"] = dt("xn2T_d", [DM, S], BF16, kind="Internal").ap()
    T["x3T_d"] = dt("x3T_d", [DFFP, S], BF16, kind="Internal").ap()
    T["y3_d"] = dt("y3_d", [S, DM], F32, kind="Internal").ap()
    T["cosin_d"] = dt("cosin_d", [S, 64], F32, kind="Internal").ap()
    T["cosrep_d"] = dt("cosrep_d", [S, 1024], F32, kind="Internal").ap()
    T["bar_i"] = dt("bar_i", [1, 8], F32, kind="Internal").ap()
    T["bar_o"] = dt("bar_o", [1, 8], F32, kind="Internal").ap()

    dbg_out = {}
    if any(d.startswith("stop_a") for d in dbg):
        dbg_out["dbg_stage"] = dt("dbg_stage", [128, 32768], F32, kind="ExternalOutput").ap()
    for name, shape, ddt in [("dbg_wg", [DM, DFFP], BF16), ("dbg_wd", [8 * D_PAD, DM], BF16),
                             ("dbg_wgsh", [512, DFF], BF16),
                             ("dbg_x1", [S, DM], F32), ("dbg_xn2T", [DM, S], BF16),
                             ("dbg_x3T", [DFFP, S], BF16), ("dbg_y3", [S, DM], F32),
                             ("dbg_cosrep", [S, 1024], F32)]:
        if name in dbg:
            dbg_out[name] = dt(name, shape, ddt, kind="ExternalOutput").ap()

    ident = st("ident", [128, 128], BF16).__enter__()
    identf = st("identf", [128, 128], F32).__enter__()
    caus = st("caus", [128, 128], F32).__enter__()
    s16 = st("s16", [128, 64], BF16).__enter__()
    s4 = st("s4", [128, 64], BF16).__enter__()
    biases = st("biases", [128, 20], F32).__enter__()
    ktb = st("ktb", [128, 1], F32).__enter__()
    cvals = st("cvals", [128, 4], F32).__enter__()
    lora_a = st("lora_a", [128, 150 * R], BF16).__enter__()

    pb = [nc.psum_tensor(f"pb{i}", [128, 512], F32).__enter__() for i in range(6)]
    pt = [nc.psum_tensor(f"pt{i}", [128, 512], BF16).__enter__() for i in range(2)]

    blk_cm = nc.Block()
    block = blk_cm.__enter__()
    E = Env(nc, block)
    E.dbg = dbg
    E.T = T
    E.dbg_out = dbg_out
    E.pb, E.pt = pb, pt
    E.consts = dict(ident=ident, identf=identf, caus=caus, s16=s16, s4=s4, biases=biases, ktb=ktb,
                    cvals=cvals, lora_a=lora_a)

    _emit_prologue(E)
    if ffn:
        _emit_dequant(E)
    if attn:
        _emit_attention(E)
    else:
        _emit_ln2_standalone(E)

    if ffn:
        with st("lorab", [128, DFFP], BF16) as lorab:
            E.consts["lorab"] = lorab

            @block.gpsimd
            def _(g):
                g.dma_start(out=lorab[0:16, 0:DFFP], in_=T["glb"][:]).then_inc(E.gp.h, 16)
                g.dma_start(out=lorab[32:48, 0:DFFP], in_=T["ulb"][:]).then_inc(E.gp.h, 16)
                g.dma_start(out=lorab[64:80, 0:DM], in_=T["dlb"][:]).then_inc(E.gp.h, 16)
                E.gp.n += 48
            E.mark["lorab_gp"] = E.gp.n
            _emit_f1(E)
            _emit_f2(E)
            _emit_f3(E)
    else:
        @block.sync
        def _(sy):
            sy.wait_ge(E.sy.h, E.sy.n)
            sy.wait_ge(E.gp.h, E.gp.n)
            sy.wait_ge(E.pe.h, E.pe.n)
            for tt in range(NT):
                sy.dma_start(out=T["out_t"][tt * 128:(tt + 1) * 128, :],
                             in_=T["x1_d"][tt * 128:(tt + 1) * 128, :]).then_inc(E.st.h, 16)
                E.st.n += 16

    for nm, src in [("dbg_wg", T["wg_f"]), ("dbg_wd", T["wd_f"]),
                    ("dbg_cosrep", T["cosrep_d"]),
                    ("dbg_wgsh", T["wg_sh"]), ("dbg_x1", T["x1_d"]),
                    ("dbg_xn2T", T["xn2T_d"]), ("dbg_x3T", T["x3T_d"]),
                    ("dbg_y3", T["y3_d"])]:
        if nm in dbg_out:
            _emit_dram_copy(E, dbg_out[nm], src)

    @block.sync
    def _(sy):
        sy.wait_ge(E.st.h, E.st.n)

    blk_cm.__exit__(None, None, None)
    return nc


_DBG_N = [0]


def _emit_dram_copy(E, dst, src):
    block = E.block
    nrow, ncol = src.shape[0], src.shape[1]
    _DBG_N[0] += 1
    with E.nc.sbuf_tensor(f"dbgcp{_DBG_N[0]}", [128, ncol], src.dtype) as tb:
        @block.sync
        def _(sy):
            for ctr in (E.sy, E.gp, E.sc, E.pe, E.ve, E.ac, E.cc):
                if ctr.n:
                    sy.wait_ge(ctr.h, ctr.n)
            for rr in range(0, nrow, 128):
                sy.dma_start(out=tb[:], in_=src[rr:rr + 128, :]).then_inc(E.sy.h, 16)
                E.sy.n += 16
                sy.wait_ge(E.sy.h, E.sy.n)
                sy.dma_start(out=dst[rr:rr + 128, :], in_=tb[:]).then_inc(E.st.h, 16)
                E.st.n += 16
                sy.wait_ge(E.st.h, E.st.n)


def _emit_prologue(E):
    nc, block, T, C = E.nc, E.block, E.T, E.consts
    with nc.sbuf_tensor("iotai", [128, 128], I32) as iotai:
        @block.gpsimd
        def _(g):
            fence(g, E.gpf, g.iota(iotai[:], [[1, 128]], base=0, channel_multiplier=-1))

        @block.vector
        def _(v):
            v.wait_ge(E.gpf.h, E.gpf.n)
            v.tensor_scalar(C["ident"][:], iotai[:], 0.0, None, ALU.is_equal)
            v.tensor_scalar(C["identf"][:], iotai[:], 0.0, None, ALU.is_equal)
            fence(v, E.ve, v.tensor_scalar(C["caus"][:], iotai[:], 0.0, -1e30, ALU.is_gt, ALU.mult))

        @block.gpsimd
        def _(g):
            g.wait_ge(E.ve.h, E.ve.n)
            fence(g, E.gpf, g.iota(iotai[:, 0:64], [[-1, 64]], base=0, channel_multiplier=1))

        @block.vector
        def _(v):
            v.wait_ge(E.gpf.h, E.gpf.n)
            v.tensor_scalar(C["s16"][:], iotai[:, 0:64], 0.0, None, ALU.is_equal)
            fence(v, E.ve, v.tensor_scalar(C["s4"][:], iotai[:, 0:64], 64.0, None, ALU.is_equal))
            fence(v, E.ve, v.tensor_tensor(C["s16"][:], C["s16"][:], C["s4"][:], ALU.add))
            v.tensor_scalar(C["s4"][:], C["s16"][:], 0.25, None, ALU.mult)
            fence(v, E.ve, v.tensor_scalar(C["s16"][:], C["s16"][:], 1.0 / 16.0, None, ALU.mult))

    @block.vector
    def _(v):
        import math
        v.memset(C["cvals"][:, 0:1], math.pi / 2)
        fence(v, E.ve, v.memset(C["cvals"][:, 1:2], math.log(0.125)))

    @block.sync
    def _(sy):
        sy.dma_start(out=C["biases"][:, 0:8], in_=T["qc1b"][:]).then_inc(E.sy.h, 16)
        sy.dma_start(out=C["biases"][:, 8:16], in_=T["qc2b"][:]).then_inc(E.sy.h, 16)
        sy.dma_start(out=C["biases"][:, 16:18], in_=T["kc1b"][:]).then_inc(E.sy.h, 16)
        sy.dma_start(out=C["biases"][:, 18:20], in_=T["kc2b"][:]).then_inc(E.sy.h, 16)
        sy.dma_start(out=C["ktb"][:], in_=T["ktemp"].to_broadcast([128, 1])).then_inc(E.sy.h, 16)
        E.sy.n += 80

    @block.gpsimd
    def _(g):
        ofs = 0
        for src, nch in [(T["gla"], 32), (T["ula"], 32), (T["dla"], 86)]:
            for ch in range(nch):
                g.dma_start(out=C["lora_a"][:, (ofs + ch) * R:(ofs + ch + 1) * R],
                            in_=src[ch * 128:(ch + 1) * 128, :]).then_inc(E.gp.h, 16)
                E.gp.n += 16
            ofs += nch
    E.mark["lora_gp"] = E.gp.n


def _emit_dequant(E):
    nc, block, T = E.nc, E.block, E.T
    JC = 2048
    mats = [(T["gcodes"], T["gabs"], T["wg_sh"], T["wg_f"], 4, DFFP),
            (T["ucodes"], T["uabs"], T["wu_sh"], T["wu_f"], 4, DFFP),
            (T["dcodes"], T["dabs"], T["wd_sh"], T["wd_f"], 11, DM)]
    with (nc.sbuf_tensor("dq_c", [128, 2 * DFFP], BF16) as cbuf,
          nc.sbuf_tensor("dq_m", [128, JC], F32) as mbuf,
          nc.sbuf_tensor("dq_b3", [128, JC], I8) as b3,
          nc.sbuf_tensor("dq_b2", [128, JC], I8) as b2,
          nc.sbuf_tensor("dq_b1", [128, JC], I8) as b1,
          nc.sbuf_tensor("dq_b0", [128, JC], BF16) as b0,
          nc.sbuf_tensor("dq_r", [128, JC], BF16) as rr_,
          nc.sbuf_tensor("dq_f", [128, 8 * JC], BF16) as fbuf,
          nc.sbuf_tensor("dq_bar", [1, 8], F32) as bmbuf):
        F = lambda k, w: fbuf[:, k * JC:k * JC + w]
        ext_done = {}
        mult_done = 0
        store_done = 0
        ti = 0

        for mi, (codes, absm, shard, full, ntile, J) in enumerate(mats):
            nchunk = (J + JC - 1) // JC
            for i in range(ntile):
                par = ti % 2
                gate = ext_done.get(ti - 2, 0)

                @block.gpsimd
                def _(g, i=i, par=par, codes=codes, J=J, gate=gate):
                    if gate:
                        g.wait_ge(E.ve.h, gate)
                    g.dma_start(out=cbuf[:, par * DFFP:par * DFFP + J],
                                in_=codes[i * 128:(i + 1) * 128, :]).then_inc(E.gp.h, 16)
                    E.gp.n += 16
                ld_codes = E.gp.n

                for jc in range(nchunk):
                    j0 = jc * JC
                    w = min(JC, J - j0)
                    Cc = cbuf[:, par * DFFP + j0: par * DFFP + j0 + w]

                    @block.sync
                    def _(sy, i=i, absm=absm, j0=j0, w=w, mult_done=mult_done):
                        if mult_done:
                            sy.wait_ge(E.ve.h, mult_done)
                        sy.dma_start(out=mbuf[0:64, 0:w],
                                     in_=absm[2 * i:2 * i + 1, j0:j0 + w].to_broadcast([64, w]))\
                          .then_inc(E.sy.h, 16)
                        sy.dma_start(out=mbuf[64:128, 0:w],
                                     in_=absm[2 * i + 1:2 * i + 2, j0:j0 + w].to_broadcast([64, w]))\
                          .then_inc(E.sy.h, 16)
                        E.sy.n += 32
                    ld_abs = E.sy.n

                    @block.vector
                    def _(v, Cc=Cc, w=w, ld_codes=ld_codes):
                        v.wait_ge(E.gp.h, ld_codes)
                        v.tensor_scalar(b3[:, 0:w], Cc, 8.0, None, ALU.is_ge)
                        v.scalar_tensor_tensor(rr_[:, 0:w], b3[:, 0:w], -8.0, Cc, ALU.mult, ALU.add)
                        v.tensor_scalar(b2[:, 0:w], rr_[:, 0:w], 4.0, None, ALU.is_ge)
                        v.scalar_tensor_tensor(rr_[:, 0:w], b2[:, 0:w], -4.0, rr_[:, 0:w], ALU.mult, ALU.add)
                        v.tensor_scalar(b1[:, 0:w], rr_[:, 0:w], 2.0, None, ALU.is_ge)
                        inst = v.scalar_tensor_tensor(b0[:, 0:w], b1[:, 0:w], -2.0, rr_[:, 0:w], ALU.mult, ALU.add)
                        fence(v, E.ve, inst)
                    if jc == nchunk - 1:
                        ext_done[ti] = E.ve.n

                    @block.vector
                    def _(v, w=w, ld_abs=ld_abs, store_done=store_done):
                        if store_done:
                            v.wait_ge(E.gp.h, store_done)
                        for k in range(8):
                            lo, hi = float(NF4[2 * k]), float(NF4[2 * k + 1])
                            inst = v.tensor_scalar(F(k, w), b0[:, 0:w], hi - lo, lo, ALU.mult, ALU.add)
                        fence(v, E.ve, inst)
                        v.copy_predicated(F(0, w), b1[:, 0:w], F(1, w))
                        v.copy_predicated(F(2, w), b1[:, 0:w], F(3, w))
                        v.copy_predicated(F(4, w), b1[:, 0:w], F(5, w))
                        fence(v, E.ve, v.copy_predicated(F(6, w), b1[:, 0:w], F(7, w)))
                        v.copy_predicated(F(0, w), b2[:, 0:w], F(2, w))
                        fence(v, E.ve, v.copy_predicated(F(4, w), b2[:, 0:w], F(6, w)))
                        fence(v, E.ve, v.copy_predicated(F(0, w), b3[:, 0:w], F(4, w)))
                        v.wait_ge(E.sy.h, ld_abs)
                        inst = v.tensor_tensor(F(0, w), F(0, w), mbuf[:, 0:w], ALU.mult)
                        fence(v, E.ve, inst)
                    mult_done = E.ve.n

                    @block.gpsimd
                    def _(g, i=i, j0=j0, w=w, shard=shard):
                        g.wait_ge(E.ve.h, E.ve.n)
                        g.dma_start(out=shard[i * 128:(i + 1) * 128, j0:j0 + w],
                                    in_=F(0, w)).then_inc(E.gp.h, 16)
                        E.gp.n += 16
                    store_done = E.gp.n
                ti += 1

        @block.gpsimd
        def _(g):
            g.wait_ge(E.gp.h, E.gp.n)
            g.wait_ge(E.sy.h, E.sy.n)
            g.wait_ge(E.ve.h, E.ve.n)
            for (codes, absm, shard, full, ntile, J) in mats:
                g.collective_compute("AllGather", ALU.bypass,
                                     replica_groups=[list(range(8))],
                                     ins=[shard.opt()], outs=[full.opt()]).then_inc(E.cc.h, 1)
                E.cc.n += 1
        E.mark["gather0"] = E.mark["gather1"] = E.mark["gather2"] = E.cc.n

        @block.gpsimd
        def _(g):
            g.memset(bmbuf[:], 1.0)
            fence(g, E.gpf, g.engine_nop())
            g.dma_start(out=T["bar_i"][:], in_=bmbuf[:]).then_inc(E.gp.h, 16)
            E.gp.n += 16
            g.wait_ge(E.gp.h, E.gp.n)
            g.collective_compute("AllReduce", ALU.add,
                                 replica_groups=[list(range(8))],
                                 ins=[T["bar_i"].opt()], outs=[T["bar_o"].opt()]).then_inc(E.cc.h, 1)
            E.cc.n += 1
        E.mark["wbarrier"] = E.cc.n


def _emit_ln2_standalone(E):
    nc, block, T = E.nc, E.block, E.T
    ident, pt = E.consts["ident"], E.pt
    with (nc.sbuf_tensor("l2_x", [128, DM], F32) as xb,
          nc.sbuf_tensor("l2_scr", [128, DM], F32) as scr,
          nc.sbuf_tensor("l2_st", [128, 4], F32) as stat,
          nc.sbuf_tensor("l2_n", [128, DM], BF16) as nb,
          nc.sbuf_tensor("l2_t", [128, DM], BF16) as tb):
        for tt in range(NT):
            @block.sync
            def _(sy, tt=tt):
                if E.cc.n:
                    sy.wait_ge(E.cc.h, E.cc.n)
                sy.wait_ge(E.gp.h, E.gp.n)
                sy.wait_ge(E.pe.h, E.pe.n)
                sy.dma_start(out=xb[:], in_=T["x1_d"][tt * 128:(tt + 1) * 128, :]).then_inc(E.sy.h, 16)
                E.sy.n += 16

            @block.vector
            def _(v):
                v.wait_ge(E.sy.h, E.sy.n)
            _ln_core(E, xb, scr, stat, DM)

            @block.vector
            def _(v):
                v.wait_ge(E.ac.h, E.ac.n)
                fence(v, E.ve, v.tensor_scalar(nb[:], xb[:], stat[:, 3:4], None, ALU.mult))

            for grp in range(8):
                bank = pt[grp % 2]

                @block.tensor
                def _(t, grp=grp, bank=bank):
                    t.wait_ge(E.ve.h, E.ve.n)
                    for k in range(4):
                        ch = grp * 4 + k
                        inst = t.transpose(bank[:, k * 128:(k + 1) * 128],
                                           nb[:, ch * 128:(ch + 1) * 128], ident[:])
                    E.pe.inc(inst)

                @block.vector
                def _(v, grp=grp, bank=bank):
                    v.wait_ge(E.pe.h, E.pe.n)
                    fence(v, E.ve, v.tensor_copy(tb[:, grp * 512:(grp + 1) * 512], bank[:]))

            @block.gpsimd
            def _(g, tt=tt):
                g.wait_ge(E.ve.h, E.ve.n)
                for ch in range(32):
                    g.dma_start(out=T["xn2T_d"][ch * 128:(ch + 1) * 128, tt * 128:(tt + 1) * 128],
                                in_=tb[:, ch * 128:(ch + 1) * 128]).then_inc(E.gp.h, 16)
                    E.gp.n += 16
    E.mark["xn2T_gp"] = E.gp.n
    E.mark["xn2T_sy"] = 0
    E.mark["x1_gp"] = 0


def _scope_barrier(E):
    block = E.block

    @block.sync
    def _(sy):
        for ctr in (E.pe, E.ve, E.ac, E.gp, E.sc):
            if ctr.n:
                sy.wait_ge(ctr.h, ctr.n)

    @block.gpsimd
    def _(g):
        for ctr in (E.pe, E.ve, E.ac, E.sy, E.sc):
            if ctr.n:
                g.wait_ge(ctr.h, ctr.n)

    @block.scalar
    def _(s):
        for ctr in (E.pe, E.ve, E.gp, E.sy, E.sc):
            if ctr.n:
                s.wait_ge(ctr.h, ctr.n)


def _dump_sbuf(E, bufs):
    block = E.block
    dst = E.dbg_out["dbg_stage"]
    col = [0]

    @block.sync
    def _(sy):
        for ctr in (E.sy, E.gp, E.sc, E.pe, E.ve, E.ac):
            if ctr.n:
                sy.wait_ge(ctr.h, ctr.n)
    @block.gpsimd
    def _(g):
        for ctr in (E.sy, E.pe, E.ve, E.ac):
            if ctr.n:
                g.wait_ge(ctr.h, ctr.n)
        for b in bufs:
            w = b.shape[-1]
            npart = b.shape[0]
            g.dma_start(out=dst[0:npart, col[0]:col[0] + w], in_=b).then_inc(E.st.h, 16)
            E.st.n += 16
            col[0] += w


def _emit_attention(E):
    import math
    nc, block, T, C = E.nc, E.block, E.T, E.consts
    pb, pt = E.pb, E.pt
    ident, identf, caus, cvals = C["ident"], C["identf"], C["caus"], C["cvals"]
    biases, ktb = C["biases"], C["ktb"]

    with (nc.sbuf_tensor("a0_i", [128, 32], I32, side="right") as ioti,
          nc.sbuf_tensor("a0_f", [128, 32], F32, side="right") as invb,
          nc.sbuf_tensor("a0_t", [128, 8], F32, side="right") as tval,
          nc.sbuf_tensor("a0_w", [128, 6 * 32], F32, side="right") as w0):
        ANG, KF, M2, AB, CO, SI = (w0[:, i * 32:(i + 1) * 32] for i in range(6))

        @block.gpsimd
        def _(g):
            fence(g, E.gpf, g.iota(ioti[:, 0:32], [[1, 32]], base=0, channel_multiplier=0))

        @block.vector
        def _(v):
            v.wait_ge(E.gpf.h, E.gpf.n)
            fence(v, E.ve, v.tensor_scalar(invb[:], ioti[:, 0:32], 1.0, None, ALU.bypass))

        @block.scalar
        def _(s):
            s.wait_ge(E.ve.h, E.ve.n)
            fence(s, E.ac, s.activation(invb[:], invb[:], AF.Exp, scale=-math.log(10000.0) / 32.0))

        for tt in range(NT):
            @block.gpsimd
            def _(g, tt=tt):
                g.wait_ge(E.sy.h, E.sy.n)
                fence(g, E.gpf, g.iota(tval[:, 0:1], [[1, 1]], base=128 * tt,
                                      channel_multiplier=1,
                                      allow_small_or_imprecise_dtypes=True))

            @block.vector
            def _(v):
                v.wait_ge(E.gpf.h, E.gpf.n)
                v.wait_ge(E.ac.h, E.ac.n)
                fence(v, E.ve, v.tensor_scalar(ANG, invb[:], tval[:, 0:1], None, ALU.mult))
                fence(v, E.ve, v.tensor_scalar(ioti[:, 0:32], ANG, 1.0 / (2 * math.pi), None, ALU.mult))
                fence(v, E.ve, v.tensor_scalar(KF, ioti[:, 0:32], 1.0, None, ALU.bypass))
                fence(v, E.ve, v.scalar_tensor_tensor(M2, KF, -2 * math.pi, ANG, ALU.mult, ALU.add))
                fence(v, E.ve, v.tensor_scalar(AB, M2, math.pi, None, ALU.is_ge))
                fence(v, E.ve, v.scalar_tensor_tensor(M2, AB, -2 * math.pi, M2, ALU.mult, ALU.add))
                fence(v, E.ve, v.tensor_scalar(AB, M2, -math.pi, None, ALU.is_lt))
                fence(v, E.ve, v.scalar_tensor_tensor(M2, AB, 2 * math.pi, M2, ALU.mult, ALU.add))
                fence(v, E.ve, v.tensor_scalar(M2, M2, math.pi, -math.pi, ALU.min, ALU.max))

            @block.scalar
            def _(s):
                s.wait_ge(E.ve.h, E.ve.n)
                fence(s, E.ac, s.activation(SI, M2, AF.Sin))
                fence(s, E.ac, s.activation(AB, M2, AF.Abs))
                fence(s, E.ac, s.activation(CO, AB, AF.Sin, scale=-1.0, bias=cvals[:, 0:1]))

            @block.sync
            def _(sy, tt=tt):
                sy.wait_ge(E.ac.h, E.ac.n)
                for h in range(16):
                    sy.dma_start(out=T["cosrep_d"][tt * 128:(tt + 1) * 128, h * 32:(h + 1) * 32],
                                 in_=CO).then_inc(E.sy.h, 16)
                    sy.dma_start(out=T["cosrep_d"][tt * 128:(tt + 1) * 128, 512 + h * 32:512 + (h + 1) * 32],
                                 in_=SI).then_inc(E.sy.h, 16)
                E.sy.n += 32 * 16

    with (nc.sbuf_tensor("at_qpreT", [128, 8 * 2052], BF16) as qpreT,
          nc.sbuf_tensor("at_kpreT", [128, 2 * 2052], BF16) as kpreT,
          nc.sbuf_tensor("at_vfm", [128, 2 * 2048], BF16) as vfm,
          nc.sbuf_tensor("at_qm", [128, NT * 64], F32) as qmean,
          nc.sbuf_tensor("at_km", [128, NT * 64], F32) as kmean,
          nc.sbuf_tensor("at_st", [128, 8], F32) as stat):

        QP = lambda ch: qpreT[:, ch * 2052:(ch + 1) * 2052]
        KP = lambda ch: kpreT[:, ch * 2052:(ch + 1) * 2052]

        @block.vector
        def _(v):
            for ch in range(8):
                v.memset(QP(ch)[:, 0:2], 0.0)
            for ch in range(2):
                v.memset(KP(ch)[:, 0:2], 0.0)
            v.memset(vfm[:, 2048:2049], 0.0)
            fence(v, E.ve, v.engine_nop())

        _scope_barrier(E)
        with (nc.sbuf_tensor("a1_w", [128, 32 * 512], BF16, side="right") as wqkv,
              nc.sbuf_tensor("a1_vw", [128, 2 * 32 * 128], BF16, side="right") as vwb,
              nc.sbuf_tensor("a1_x", [128, DM], F32, side="right") as xb,
              nc.sbuf_tensor("a1_n", [128, DM], BF16, side="right") as nb,
              nc.sbuf_tensor("a1_xt", [128, 32 * 512], BF16, side="right") as xnT):
            @block.gpsimd
            def _(g):
                for ch in range(32):
                    g.dma_start(out=wqkv[:, ch * 512:ch * 512 + 512],
                                in_=T["wqkvT"][ch * 128:(ch + 1) * 128, 0:512]).then_inc(E.gp.h, 16)
                    E.gp.n += 16
            wq_done = E.gp.n

            def load_oc_slice(g, oc, gate):
                if gate:
                    g.wait_ge(E.pe.h, gate)
                slot = oc % 2
                g.dma_start(out=vwb[:, slot * 4096:(slot + 1) * 4096],
                            in_=T["wqkvT"][:, oc * 128:(oc + 1) * 128]
                            .rearrange("(c p) j -> p c j", p=128)).then_inc(E.gp.h, 16)
                E.gp.n += 16
                return E.gp.n

            ev_oc = []
            for tg in range(4):
                t0 = tg * 512
                for sub in range(4):
                    tt = tg * 4 + sub

                    @block.sync
                    def _(sy, tt=tt):
                        sy.wait_ge(E.pe.h, E.pe.n)
                        sy.dma_start(out=xb[:], in_=T["x_in"][tt * 128:(tt + 1) * 128, :]).then_inc(E.sy.h, 16)
                        E.sy.n += 16

                    @block.vector
                    def _(v):
                        v.wait_ge(E.sy.h, E.sy.n)
                    _ln_core(E, xb, nb, stat, DM)

                    @block.vector
                    def _(v):
                        v.wait_ge(E.ac.h, E.ac.n)
                        fence(v, E.ve, v.tensor_scalar(nb[:], xb[:], stat[:, 3:4], None, ALU.mult))

                    for grp in range(8):
                        bank = pt[grp % 2]

                        @block.tensor
                        def _(t, grp=grp, bank=bank):
                            t.wait_ge(E.ve.h, E.ve.n)
                            for k in range(4):
                                ch = grp * 4 + k
                                inst = t.transpose(bank[:, k * 128:(k + 1) * 128],
                                                   nb[:, ch * 128:(ch + 1) * 128], ident[:])
                            E.pe.inc(inst)

                        @block.vector
                        def _(v, grp=grp, bank=bank, sub=sub):
                            v.wait_ge(E.pe.h, E.pe.n)
                            for k in range(4):
                                ch = grp * 4 + k
                                inst = v.tensor_copy(xnT[:, ch * 512 + sub * 128:ch * 512 + sub * 128 + 128],
                                                     bank[:, k * 128:(k + 1) * 128])
                            fence(v, E.ve, inst)
                xn_done = E.ve.n

                ld_ok = {}
                @block.gpsimd
                def _(g, gate=E.mark.get("a1_strm_pe", 0)):
                    ld_ok[4] = load_oc_slice(g, 4, gate)
                    ld_ok[5] = load_oc_slice(g, 5, gate)

                pe_oc = {}
                for oc in range(12):
                    bank = pb[oc % 4]
                    gate_ac = ev_oc[-4] if len(ev_oc) >= 4 else 0

                    @block.tensor
                    def _(t, oc=oc, bank=bank, xn_done=xn_done, wq_done=wq_done,
                          gate_ac=gate_ac, lo=ld_ok.get(oc, 0)):
                        t.wait_ge(E.ve.h, xn_done)
                        t.wait_ge(E.gp.h, wq_done if oc < 4 else lo)
                        if gate_ac:
                            t.wait_ge(E.ac.h, gate_ac)
                        for ch in range(32):
                            if oc < 4:
                                lw = wqkv[:, ch * 512 + oc * 128:ch * 512 + (oc + 1) * 128]
                            else:
                                lw = vwb[:, (oc % 2) * 4096 + ch * 128:(oc % 2) * 4096 + (ch + 1) * 128]
                            inst = t.matmul(bank[:], lw, xnT[:, ch * 512:(ch + 1) * 512],
                                            start=(ch == 0), stop=(ch == 31))
                        E.pe.inc(inst)
                    pe_oc[oc] = E.pe.n
                    if 4 <= oc <= 9:
                        @block.gpsimd
                        def _(g, oc=oc, gate=pe_oc[oc]):
                            ld_ok[oc + 2] = load_oc_slice(g, oc + 2, gate)

                    @block.scalar
                    def _(s, oc=oc, bank=bank, t0=t0):
                        s.wait_ge(E.pe.h, E.pe.n)
                        if oc < 8:
                            inst = s.activation(QP(oc)[:, 2 + t0:2 + t0 + 512], bank[:], AF.Copy)
                        elif oc < 10:
                            inst = s.activation(KP(oc - 8)[:, 2 + t0:2 + t0 + 512], bank[:], AF.Copy)
                        elif oc == 10:
                            inst = s.activation(vfm[:, t0:t0 + 512], bank[:], AF.Copy)
                        else:
                            w = 512 if t0 + 512 < S else 511
                            inst = s.activation(vfm[:, 2048 + 1 + t0:2048 + 1 + t0 + w],
                                                bank[:, 0:w], AF.Copy)
                        E.ac.inc(inst)
                    ev_oc.append(E.ac.n)
                E.mark["a1_strm_pe"] = E.pe.n

                @block.tensor
                def _(t, t0=t0):
                    t.wait_ge(E.ac.h, E.ac.n)
                    t.wait_ge(E.ve.h, E.ve.n)
                    for ch in range(8):
                        t.matmul(pb[4][0:64, :], C["s16"][:], QP(ch)[:, 2 + t0:2 + t0 + 512],
                                 start=(ch == 0), stop=(ch == 7))
                    for ch in range(2):
                        inst = t.matmul(pb[5][0:64, :], C["s4"][:], KP(ch)[:, 2 + t0:2 + t0 + 512],
                                        start=(ch == 0), stop=(ch == 1))
                    E.pe.inc(inst)

                @block.vector
                def _(v):
                    v.wait_ge(E.pe.h, E.pe.n)
                    v.tensor_copy(xb[0:64, 0:512], pb[4][0:64, :])
                    fence(v, E.ve, v.tensor_copy(xb[0:64, 512:1024], pb[5][0:64, :]))

                for sub in range(4):
                    @block.tensor
                    def _(t, sub=sub):
                        t.wait_ge(E.ve.h, E.ve.n)
                        dv = pt[0].bitcast(F32)
                        t.transpose(dv[:, 0:64], xb[0:64, sub * 128:(sub + 1) * 128], identf[0:64, 0:64])
                        inst = t.transpose(dv[:, 64:128], xb[0:64, 512 + sub * 128:512 + (sub + 1) * 128],
                                           identf[0:64, 0:64])
                        E.pe.inc(inst)

                    @block.vector
                    def _(v, sub=sub, tg=tg):
                        v.wait_ge(E.pe.h, E.pe.n)
                        tt = tg * 4 + sub
                        sv = pt[0].bitcast(F32)
                        v.tensor_copy(qmean[:, tt * 64:(tt + 1) * 64], sv[:, 0:64])
                        inst = v.tensor_copy(kmean[:, tt * 64:(tt + 1) * 64], sv[:, 64:128])
                        fence(v, E.ve, inst)

        if "stop_a1" in E.dbg:
            _dump_sbuf(E, [qpreT[:], kpreT[:], vfm[:], qmean[:], kmean[:]])
            return

        _scope_barrier(E)
        with (nc.sbuf_tensor("a2_c1q", [128, 8 * 2052], BF16, side="right") as c1q,
              nc.sbuf_tensor("a2_c1k", [128, 2 * 2052], BF16, side="right") as c1k,
              nc.sbuf_tensor("a2_w1", [128, 30 * 64], BF16, side="right") as w1,
              nc.sbuf_tensor("a2_w2", [128, 3 * 8 * 1024 + 3 * 2 * 256], BF16, side="right") as w2):
            C1Q = lambda ch: c1q[:, ch * 2052:(ch + 1) * 2052]
            C1K = lambda ch: c1k[:, ch * 2052:(ch + 1) * 2052]
            KW2 = 3 * 8 * 1024

            @block.vector
            def _(v):
                for ch in range(8):
                    v.memset(C1Q(ch)[:, 0:2], 0.0)
                for ch in range(2):
                    v.memset(C1K(ch)[:, 0:2], 0.0)
                fence(v, E.ve, v.engine_nop())

            @block.gpsimd
            def _(g):
                for tap in range(3):
                    for gg in range(16):
                        p, r0 = gg // 2, (gg % 2) * 64
                        g.dma_start(out=w1[r0:r0 + 64, (tap * 8 + p) * 64:(tap * 8 + p + 1) * 64],
                                    in_=T["qc1T"][tap, gg, :, :]).then_inc(E.gp.h, 16)
                        E.gp.n += 16
                    for gg in range(4):
                        p, r0 = gg // 2, (gg % 2) * 64
                        g.dma_start(out=w1[r0:r0 + 64, (24 + tap * 2 + p) * 64:(24 + tap * 2 + p + 1) * 64],
                                    in_=T["kc1T"][tap, gg, :, :]).then_inc(E.gp.h, 16)
                        E.gp.n += 16
                    for ic in range(8):
                        g.dma_start(out=w2[:, (tap * 8 + ic) * 1024:(tap * 8 + ic + 1) * 1024],
                                    in_=T["qc2T"][tap, ic * 128:(ic + 1) * 128, :]).then_inc(E.gp.h, 16)
                        E.gp.n += 16
                    for ic in range(2):
                        g.dma_start(out=w2[:, KW2 + (tap * 2 + ic) * 256:KW2 + (tap * 2 + ic + 1) * 256],
                                    in_=T["kc2T"][tap, ic * 128:(ic + 1) * 128, :]).then_inc(E.gp.h, 16)
                        E.gp.n += 16
            w_conv = E.gp.n

            for (npair, SRC, DST, coff, bbase, ntap) in ((8, QP, C1Q, 0, 0, 8), (2, KP, C1K, 24, 16, 2)):
                for tg in range(4):
                    t0 = tg * 512
                    for p in range(npair):
                        bank = pb[(tg * npair + p) % 4]

                        @block.tensor
                        def _(t, p=p, t0=t0, bank=bank, SRC=SRC, coff=coff, ntap=ntap, w_conv=w_conv):
                            t.wait_ge(E.gp.h, w_conv)
                            t.wait_ge(E.ac.h, E.ac.n)
                            for tap in range(3):
                                wc = (coff + tap * ntap + p) * 64
                                rhs = SRC(p)[:, t0 + tap:t0 + tap + 512]
                                t.matmul(bank[0:64, :], w1[0:64, wc:wc + 64], rhs[0:64, :],
                                         start=(tap == 0), stop=(tap == 2))
                            for tap in range(3):
                                wc = (coff + tap * ntap + p) * 64
                                rhs = SRC(p)[:, t0 + tap:t0 + tap + 512]
                                inst = t.matmul(bank[64:128, :], w1[64:128, wc:wc + 64], rhs[64:128, :],
                                                start=(tap == 0), stop=(tap == 2),
                                                tile_position=(64, 64))
                            E.pe.inc(inst)

                        @block.vector
                        def _(v, p=p, t0=t0, bank=bank, DST=DST, bbase=bbase):
                            v.wait_ge(E.pe.h, E.pe.n)
                            inst = v.tensor_scalar(DST(p)[:, 2 + t0:2 + t0 + 512], bank[:],
                                                   biases[:, bbase + p:bbase + p + 1], None, ALU.add)
                            E.ac.inc(inst)

            for (nic, SRC, DST, w2off, bbase) in ((8, C1Q, QP, 0, 8), (2, C1K, KP, KW2, 18)):
                wid = nic * 128
                for tg in range(4):
                    t0 = tg * 512
                    for oc in range(nic):
                        bank = pb[(tg * nic + oc) % 4]

                        @block.tensor
                        def _(t, oc=oc, t0=t0, bank=bank, nic=nic, SRC=SRC, w2off=w2off, wid=wid):
                            t.wait_ge(E.ac.h, E.ac.n)
                            for tap in range(3):
                                for ic in range(nic):
                                    rhs = SRC(ic)[:, t0 + tap:t0 + tap + 512]
                                    inst = t.matmul(bank[:],
                                                    w2[:, w2off + (tap * nic + ic) * wid + oc * 128:
                                                       w2off + (tap * nic + ic) * wid + oc * 128 + 128],
                                                    rhs, start=(tap == 0 and ic == 0),
                                                    stop=(tap == 2 and ic == nic - 1))
                            E.pe.inc(inst)

                        @block.vector
                        def _(v, oc=oc, t0=t0, bank=bank, DST=DST, bbase=bbase):
                            v.wait_ge(E.pe.h, E.pe.n)
                            inst = v.tensor_scalar(DST(oc)[:, 2 + t0:2 + t0 + 512], bank[:],
                                                   biases[:, bbase + oc:bbase + oc + 1], None, ALU.add)
                            E.ac.inc(inst)
        conv_ac = E.ac.n
        if "stop_a2" in E.dbg:
            _dump_sbuf(E, [qpreT[:], kpreT[:]])
            return

        _scope_barrier(E)
        qt_cms = [nc.sbuf_tensor("a3_QT", [128, 8 * 2048], BF16, side="right"),
                  nc.sbuf_tensor("a3_KT", [128, 2 * 2048], BF16, side="right"),
                  nc.sbuf_tensor("a3_V", [128, NT * 256], BF16, side="right")]
        QT = qt_cms[0].__enter__()
        KT = qt_cms[1].__enter__()
        VT = qt_cms[2].__enter__()
        with (nc.sbuf_tensor("a3_cs", [128, 1024], F32, side="right") as csb,
              nc.sbuf_tensor("a3_tm", [128, 1024], BF16, side="right") as tmb,
              nc.sbuf_tensor("a3_qo", [128, 1024 + 256], BF16, side="right") as qko,
              nc.sbuf_tensor("a3_wk", [128, 3200], F32, side="right") as wk,
              nc.sbuf_tensor("a3_ss", [128, 48], F32, side="right") as ssb):
            KH = wk[:, 0:64]
            QH = wk[:, 64:128]
            SQ = wk[:, 128:1152]
            T1 = wk[:, 1152:1664]
            T2 = wk[:, 1664:2176]
            QW = wk[:, 2176:3200]

            a3lim = NT
            for d in E.dbg:
                if d.startswith("a3lim"):
                    a3lim = int(d[5:])
            for tt in range(a3lim):
                t0 = tt * 128

                @block.sync
                def _(sy, tt=tt):
                    sy.wait_ge(E.ve.h, E.ve.n)
                    sy.dma_start(out=csb[:], in_=T["cosrep_d"][tt * 128:(tt + 1) * 128, :]).then_inc(E.sy.h, 16)
                    E.sy.n += 16
                cs_ok = E.sy.n

                @block.tensor
                def _(t, t0=t0, conv_ac=conv_ac):
                    t.wait_ge(E.ac.h, conv_ac)
                    t.wait_ge(E.ve.h, E.ve.n)
                    for grp in range(2):
                        for k in range(4):
                            ch = grp * 4 + k
                            inst = t.transpose(pt[grp][:, k * 128:(k + 1) * 128],
                                               QP(ch)[:, 2 + t0:2 + t0 + 128], ident[:])
                        E.pe.inc(inst)

                @block.vector
                def _(v):
                    v.wait_ge(E.pe.h, E.pe.n)
                    v.tensor_copy(tmb[:, 0:512], pt[0][:])
                    fence(v, E.ve, v.tensor_copy(tmb[:, 512:1024], pt[1][:]))

                @block.vector
                def _(v, tt=tt):
                    fence(v, E.ve, v.tensor_scalar(KH, kmean[:, tt * 64:(tt + 1) * 64], 0.5, None, ALU.mult))
                    for h in range(16):
                        inst = v.tensor_tensor(QW[:, h * 64:(h + 1) * 64],
                                               tmb[:, h * 64:(h + 1) * 64], KH, ALU.add)
                    fence(v, E.ve, inst)

                @block.scalar
                def _(s):
                    s.wait_ge(E.ve.h, E.ve.n)
                    inst = s.activation(SQ, QW, AF.Square)
                    E.ac.inc(inst)

                @block.vector
                def _(v):
                    v.wait_ge(E.ac.h, E.ac.n)
                    for h in range(16):
                        inst = v.tensor_reduce(ssb[:, h:h + 1], SQ[:, h * 64:(h + 1) * 64], AXX, ALU.add)
                    fence(v, E.ve, inst)
                    fence(v, E.ve, v.tensor_scalar(ssb[:, 0:16], ssb[:, 0:16], 1e-24, None, ALU.max))

                @block.scalar
                def _(s):
                    s.wait_ge(E.ve.h, E.ve.n)
                    fence(s, E.ac, s.activation(ssb[:, 16:32], ssb[:, 0:16], AF.Ln))
                    fence(s, E.ac, s.activation(ssb[:, 16:32], ssb[:, 16:32], AF.Exp,
                                                scale=-0.5, bias=cvals[:, 1:2]))

                @block.vector
                def _(v):
                    v.wait_ge(E.ac.h, E.ac.n)
                    for h in range(16):
                        inst = v.tensor_scalar(QW[:, h * 64:(h + 1) * 64], QW[:, h * 64:(h + 1) * 64],
                                               ssb[:, 16 + h:17 + h], None, ALU.mult)
                    fence(v, E.ve, inst)

                q3 = QW.rearrange("p (f t) -> p f t", t=2)
                o3 = qko[:, 0:1024].rearrange("p (f t) -> p f t", t=2)
                cosv = csb[:, 0:512].unsqueeze(2)
                sinv = csb[:, 512:1024].unsqueeze(2)

                @block.vector
                def _(v, cs_ok=cs_ok):
                    v.wait_ge(E.sy.h, cs_ok)
                    v.tensor_tensor(T1.unsqueeze(2), q3[:, :, 0:1], cosv, ALU.mult)
                    fence(v, E.ve, v.tensor_tensor(T2.unsqueeze(2), q3[:, :, 1:2], sinv, ALU.mult))
                    fence(v, E.ve, v.tensor_tensor(o3[:, :, 0:1], T1.unsqueeze(2), T2.unsqueeze(2), ALU.subtract))
                    v.tensor_tensor(T1.unsqueeze(2), q3[:, :, 0:1], sinv, ALU.mult)
                    fence(v, E.ve, v.tensor_tensor(T2.unsqueeze(2), q3[:, :, 1:2], cosv, ALU.mult))
                    fence(v, E.ve, v.tensor_tensor(o3[:, :, 1:2], T1.unsqueeze(2), T2.unsqueeze(2), ALU.add))

                if tt == 0 and "stop_a3t0" in E.dbg:
                    _dump_sbuf(E, [qko[:, 0:1024], csb[:], QW])
                    return

                for c, (hA, hB) in enumerate(PAIRS):
                    bank = pt[c % 2]

                    @block.tensor
                    def _(t, hA=hA, hB=hB, bank=bank):
                        t.wait_ge(E.ve.h, E.ve.n)
                        t.transpose(bank[0:64, 0:128], qko[:, hA * 64:(hA + 1) * 64], ident[:])
                        inst = t.transpose(bank[0:64, 128:256], qko[:, hB * 64:(hB + 1) * 64], ident[:])
                        E.pe.inc(inst)

                    @block.vector
                    def _(v, c=c, bank=bank, t0=t0):
                        v.wait_ge(E.pe.h, E.pe.n)
                        v.tensor_copy(QT[0:64, c * 2048 + t0:c * 2048 + t0 + 128], bank[0:64, 0:128])
                        inst = v.tensor_copy(QT[64:128, c * 2048 + t0:c * 2048 + t0 + 128], bank[0:64, 128:256])
                        fence(v, E.ve, inst)

                @block.tensor
                def _(t, t0=t0):
                    t.wait_ge(E.ve.h, E.ve.n)
                    t.transpose(pt[0][:, 0:128], KP(0)[:, 2 + t0:2 + t0 + 128], ident[:])
                    inst = t.transpose(pt[0][:, 128:256], KP(1)[:, 2 + t0:2 + t0 + 128], ident[:])
                    E.pe.inc(inst)

                @block.vector
                def _(v):
                    v.wait_ge(E.pe.h, E.pe.n)
                    fence(v, E.ve, v.tensor_copy(tmb[:, 0:256], pt[0][:, 0:256]))

                @block.vector
                def _(v, tt=tt):
                    fence(v, E.ve, v.tensor_scalar(QH, qmean[:, tt * 64:(tt + 1) * 64], 0.5, None, ALU.mult))
                    for h in range(4):
                        inst = v.tensor_tensor(QW[:, h * 64:(h + 1) * 64],
                                               tmb[:, h * 64:(h + 1) * 64], QH, ALU.add)
                    fence(v, E.ve, inst)

                @block.scalar
                def _(s):
                    s.wait_ge(E.ve.h, E.ve.n)
                    inst = s.activation(SQ[:, 0:256], QW[:, 0:256], AF.Square)
                    E.ac.inc(inst)

                @block.vector
                def _(v):
                    v.wait_ge(E.ac.h, E.ac.n)
                    for h in range(4):
                        inst = v.tensor_reduce(ssb[:, 32 + h:33 + h], SQ[:, h * 64:(h + 1) * 64], AXX, ALU.add)
                    fence(v, E.ve, inst)
                    fence(v, E.ve, v.tensor_scalar(ssb[:, 32:36], ssb[:, 32:36], 1e-24, None, ALU.max))

                @block.scalar
                def _(s):
                    s.wait_ge(E.ve.h, E.ve.n)
                    fence(s, E.ac, s.activation(ssb[:, 36:40], ssb[:, 32:36], AF.Ln))
                    fence(s, E.ac, s.activation(ssb[:, 36:40], ssb[:, 36:40], AF.Exp, scale=-0.5))

                @block.vector
                def _(v):
                    v.wait_ge(E.ac.h, E.ac.n)
                    fence(v, E.ve, v.tensor_scalar(ssb[:, 36:40], ssb[:, 36:40], ktb[:, 0:1], None, ALU.mult))
                    for h in range(4):
                        inst = v.tensor_scalar(QW[:, h * 64:(h + 1) * 64], QW[:, h * 64:(h + 1) * 64],
                                               ssb[:, 36 + h:37 + h], None, ALU.mult)
                    fence(v, E.ve, inst)

                k3 = QW[:, 0:256].rearrange("p (f t) -> p f t", t=2)
                ko3 = qko[:, 1024:1280].rearrange("p (f t) -> p f t", t=2)
                cosk = csb[:, 0:128].unsqueeze(2)
                sink = csb[:, 512:640].unsqueeze(2)

                @block.vector
                def _(v, cs_ok=cs_ok):
                    v.wait_ge(E.sy.h, cs_ok)
                    v.tensor_tensor(T1[:, 0:128].unsqueeze(2), k3[:, :, 0:1], cosk, ALU.mult)
                    fence(v, E.ve, v.tensor_tensor(T2[:, 0:128].unsqueeze(2), k3[:, :, 1:2], sink, ALU.mult))
                    fence(v, E.ve, v.tensor_tensor(ko3[:, :, 0:1], T1[:, 0:128].unsqueeze(2),
                                                   T2[:, 0:128].unsqueeze(2), ALU.subtract))
                    v.tensor_tensor(T1[:, 0:128].unsqueeze(2), k3[:, :, 0:1], sink, ALU.mult)
                    fence(v, E.ve, v.tensor_tensor(T2[:, 0:128].unsqueeze(2), k3[:, :, 1:2], cosk, ALU.mult))
                    fence(v, E.ve, v.tensor_tensor(ko3[:, :, 1:2], T1[:, 0:128].unsqueeze(2),
                                                   T2[:, 0:128].unsqueeze(2), ALU.add))

                @block.tensor
                def _(t):
                    t.wait_ge(E.ve.h, E.ve.n)
                    for g in range(4):
                        inst = t.transpose(pt[1][0:64, g * 128:(g + 1) * 128],
                                           qko[:, 1024 + g * 64:1024 + (g + 1) * 64], ident[:])
                    E.pe.inc(inst)

                @block.vector
                def _(v, t0=t0):
                    v.wait_ge(E.pe.h, E.pe.n)
                    for g in range(4):
                        inst = v.tensor_copy(KT[(g % 2) * 64:(g % 2) * 64 + 64,
                                                (g // 2) * 2048 + t0:(g // 2) * 2048 + t0 + 128],
                                             pt[1][0:64, g * 128:(g + 1) * 128])
                    fence(v, E.ve, inst)

                @block.tensor
                def _(t, t0=t0):
                    t.wait_ge(E.ve.h, E.ve.n)
                    for b in range(2):
                        inst = t.transpose(pt[0][:, b * 128:(b + 1) * 128],
                                           vfm[:, b * 2048 + t0:b * 2048 + t0 + 128], ident[:])
                    E.pe.inc(inst)

                @block.vector
                def _(v, tt=tt):
                    v.wait_ge(E.pe.h, E.pe.n)
                    fence(v, E.ve, v.tensor_copy(VT[:, tt * 256:(tt + 1) * 256], pt[0][:, 0:256]))

        if "stop_a3" in E.dbg:
            _dump_sbuf(E, [QT[:], KT[:], VT[:]])
            return

        fm_cm = nc.sbuf_tensor("a4_fm", [128, 8 * 2048], BF16)
        attn_fm = fm_cm.__enter__()
        _emit_a4(E, QT, KT, VT, attn_fm)
        for cm in reversed(qt_cms):
            cm.__exit__(None, None, None)
        if "stop_a4" in E.dbg:
            _dump_sbuf(E, [attn_fm[:]])
            fm_cm.__exit__(None, None, None)
            return
        _emit_a5(E, attn_fm, stat)
        fm_cm.__exit__(None, None, None)


def _emit_a4(E, QT, KT, VT, attn_fm):
    nc, block, T, C = E.nc, E.block, E.T, E.consts
    pb, pt = E.pb, E.pt
    ident, caus = C["ident"], C["caus"]

    if True:
        with (nc.sbuf_tensor("a4_sr", [128, 2 * 2048], F32) as srow,
              nc.sbuf_tensor("a4_pr", [128, 2 * 2048], BF16) as prb,
              nc.sbuf_tensor("a4_pt", [128, 2 * 16 * 512], BF16) as ptb,
              nc.sbuf_tensor("a4_sm", [128, 8], F32) as smb):
            SR = lambda m: srow[:, m * 2048:(m + 1) * 2048]
            PR = lambda m: prb[:, m * 2048:(m + 1) * 2048]
            PT = lambda m: ptb[:, m * 16 * 512:(m + 1) * 16 * 512]

            scev = []
            for qg in range(4):
                nkc = 4 * qg + 4
                for c, (hA, hB) in enumerate(PAIRS):
                    kvc = c // 4
                    for sq in range(4):
                        qt = qg * 4 + sq
                        kmax = (qt + 1) * 128
                        nk512 = (kmax + 511) // 512
                        for kc in range(nk512):
                            w = min(512, kmax - kc * 512)
                            bA = pb[2 * (kc % 2)]
                            bB = pb[2 * (kc % 2) + 1]

                            @block.tensor
                            def _(t, c=c, qt=qt, kc=kc, w=w, bA=bA, bB=bB, kvc=kvc,
                                  gac=(scev[-2] if len(scev) >= 2 else 0)):
                                if gac:
                                    t.wait_ge(E.ac.h, gac)
                                t.matmul(bA[:, 0:w], QT[0:64, c * 2048 + qt * 128:c * 2048 + (qt + 1) * 128],
                                         KT[0:64, kvc * 2048 + kc * 512:kvc * 2048 + kc * 512 + w],
                                         start=True, stop=True)
                                inst = t.matmul(bB[:, 0:w],
                                                QT[64:128, c * 2048 + qt * 128:c * 2048 + (qt + 1) * 128],
                                                KT[64:128, kvc * 2048 + kc * 512:kvc * 2048 + kc * 512 + w],
                                                start=True, stop=True, tile_position=(64, 0))
                                E.pe.inc(inst)

                            @block.scalar
                            def _(s, kc=kc, w=w, bA=bA, bB=bB):
                                s.wait_ge(E.pe.h, E.pe.n)
                                s.activation(SR(0)[:, kc * 512:kc * 512 + w], bA[:, 0:w], AF.Copy)
                                inst = s.activation(SR(1)[:, kc * 512:kc * 512 + w], bB[:, 0:w], AF.Copy)
                                E.ac.inc(inst)
                            scev.append(E.ac.n)

                        for m in range(2):
                            @block.vector
                            def _(v, m=m, qt=qt, kmax=kmax):
                                v.wait_ge(E.ac.h, E.ac.n)
                                v.wait_ge(E.pe.h, E.pe.n)
                                fence(v, E.ve, v.tensor_tensor(SR(m)[:, qt * 128:kmax],
                                                               SR(m)[:, qt * 128:kmax], caus[:], ALU.add))
                                fence(v, E.ve, v.tensor_reduce(smb[:, m:m + 1], SR(m)[:, 0:kmax],
                                                               AXX, ALU.max, negate=True))

                            @block.scalar
                            def _(s, m=m, kmax=kmax):
                                s.wait_ge(E.ve.h, E.ve.n)
                                inst = s.activation(PR(m)[:, 0:kmax], SR(m)[:, 0:kmax], AF.Exp,
                                                    bias=smb[:, m:m + 1], accum_out=smb[:, 2 + m:3 + m])
                                E.ac.inc(inst)

                            @block.vector
                            def _(v, m=m, kmax=kmax):
                                v.wait_ge(E.ac.h, E.ac.n)
                                fence(v, E.ve, v.reciprocal(smb[:, 4 + m:5 + m], smb[:, 2 + m:3 + m]))
                                inst = v.tensor_scalar(PR(m)[:, 0:kmax], PR(m)[:, 0:kmax],
                                                       smb[:, 4 + m:5 + m], None, ALU.mult)
                                fence(v, E.ve, inst)

                            for kcg in range((qt + 1 + 3) // 4):
                                bank = pt[kcg % 2]
                                nk = min(4, qt + 1 - kcg * 4)

                                @block.tensor
                                def _(t, m=m, kcg=kcg, nk=nk, bank=bank):
                                    t.wait_ge(E.ve.h, E.ve.n)
                                    for j in range(nk):
                                        inst = t.transpose(bank[:, j * 128:(j + 1) * 128],
                                                           PR(m)[:, (kcg * 4 + j) * 128:(kcg * 4 + j + 1) * 128],
                                                           ident[:])
                                    E.pe.inc(inst)

                                @block.vector
                                def _(v, m=m, sq=sq, kcg=kcg, nk=nk, bank=bank):
                                    v.wait_ge(E.pe.h, E.pe.n)
                                    dst = PT(m).rearrange("p (kc j) -> p kc j", j=512)
                                    inst = v.tensor_copy(
                                        dst[:, kcg * 4:kcg * 4 + nk, sq * 128:(sq + 1) * 128],
                                        bank[:, 0:nk * 128].rearrange("p (kc j) -> p kc j", j=128))
                                    fence(v, E.ve, inst)

                    for m in range(2):
                        @block.vector
                        def _(v, m=m, qg=qg, nkc=nkc):
                            inst = None
                            for sq in range(4):
                                qt = qg * 4 + sq
                                for kc in range(qt + 1, nkc):
                                    inst = v.memset(PT(m)[:, kc * 512 + sq * 128:kc * 512 + (sq + 1) * 128], 0.0)
                            if inst is not None:
                                fence(v, E.ve, inst)

                    @block.tensor
                    def _(t, hA=hA, hB=hB, nkc=nkc):
                        t.wait_ge(E.ve.h, E.ve.n)
                        t.wait_ge(E.ac.h, E.ac.n)
                        gA, gB = hA // 4, hB // 4
                        for kc in range(nkc):
                            vA = VT[:, kc * 256 + (gA // 2) * 128 + (gA % 2) * 64:
                                    kc * 256 + (gA // 2) * 128 + (gA % 2) * 64 + 64]
                            vB = VT[:, kc * 256 + (gB // 2) * 128 + (gB % 2) * 64:
                                    kc * 256 + (gB // 2) * 128 + (gB % 2) * 64 + 64]
                            t.matmul(pb[4][0:64, :], vA, PT(0)[:, kc * 512:(kc + 1) * 512],
                                     start=(kc == 0), stop=(kc == nkc - 1))
                            inst = t.matmul(pb[4][64:128, :], vB, PT(1)[:, kc * 512:(kc + 1) * 512],
                                            start=(kc == 0), stop=(kc == nkc - 1),
                                            tile_position=(0, 64))
                        E.pe.inc(inst)

                    @block.scalar
                    def _(s, c=c, qg=qg):
                        s.wait_ge(E.pe.h, E.pe.n)
                        inst = s.activation(attn_fm[:, c * 2048 + qg * 512:c * 2048 + (qg + 1) * 512],
                                            pb[4][:], AF.Copy)
                        E.ac.inc(inst)



def _emit_a5(E, attn_fm, stat):
    nc, block, T, C = E.nc, E.block, E.T, E.consts
    pb, pt = E.pb, E.pt
    ident = C["ident"]
    _scope_barrier(E)
    if True:
        with (nc.sbuf_tensor("a5_wo", [128, 8 * 4096], BF16) as wob,
              nc.sbuf_tensor("a5_y", [128, DM], F32) as yb,
              nc.sbuf_tensor("a5_x", [128, DM], F32) as xb,
              nc.sbuf_tensor("a5_n", [128, DM], BF16) as nb2,
              nc.sbuf_tensor("a5_t", [128, DM], BF16) as tb2):
            @block.gpsimd
            def _(g):
                for ch in range(8):
                    g.dma_start(out=wob[:, ch * 4096:(ch + 1) * 4096],
                                in_=T["woTp"][ch * 128:(ch + 1) * 128, :]).then_inc(E.gp.h, 16)
                    E.gp.n += 16
            wo_ok = E.gp.n

            for tt in range(NT):
                t0 = tt * 128

                @block.sync
                def _(sy, tt=tt):
                    sy.wait_ge(E.ve.h, E.ve.n)
                    sy.wait_ge(E.gp.h, E.gp.n)
                    sy.dma_start(out=xb[:], in_=T["x_in"][tt * 128:(tt + 1) * 128, :]).then_inc(E.sy.h, 16)
                    E.sy.n += 16
                x_ok = E.sy.n

                woev = []
                for mg in range(8):
                    bank = pb[mg % 4]

                    @block.tensor
                    def _(t, mg=mg, t0=t0, bank=bank, wo_ok=wo_ok,
                          gac=(woev[-4] if len(woev) >= 4 else 0)):
                        t.wait_ge(E.gp.h, wo_ok)
                        t.wait_ge(E.ac.h, gac if gac else E.ac.n)
                        for ch in range(8):
                            inst = t.matmul(bank[:], attn_fm[:, ch * 2048 + t0:ch * 2048 + t0 + 128],
                                            wob[:, ch * 4096 + mg * 512:ch * 4096 + (mg + 1) * 512],
                                            start=(ch == 0), stop=(ch == 7))
                        E.pe.inc(inst)

                    @block.scalar
                    def _(s, mg=mg, bank=bank):
                        s.wait_ge(E.pe.h, E.pe.n)
                        inst = s.activation(yb[:, mg * 512:(mg + 1) * 512], bank[:], AF.Copy)
                        E.ac.inc(inst)
                    woev.append(E.ac.n)

                @block.vector
                def _(v):
                    v.wait_ge(E.ac.h, E.ac.n)
                _ln_core(E, yb, nb2, stat, DM)

                @block.vector
                def _(v, x_ok=x_ok):
                    v.wait_ge(E.ac.h, E.ac.n)
                    v.wait_ge(E.sy.h, x_ok)
                    fence(v, E.ve, v.tensor_scalar(yb[:], yb[:], stat[:, 3:4], None, ALU.mult))
                    fence(v, E.ve, v.tensor_tensor(xb[:], xb[:], yb[:], ALU.add))

                @block.gpsimd
                def _(g, tt=tt):
                    g.wait_ge(E.ve.h, E.ve.n)
                    g.dma_start(out=T["x1_d"][tt * 128:(tt + 1) * 128, :], in_=xb[:]).then_inc(E.gp.h, 16)
                    E.gp.n += 16

                _ln_core(E, xb, nb2, stat, DM)

                @block.vector
                def _(v):
                    v.wait_ge(E.ac.h, E.ac.n)
                    fence(v, E.ve, v.tensor_scalar(nb2[:], xb[:], stat[:, 3:4], None, ALU.mult))

                for grp in range(8):
                    bank = pt[grp % 2]

                    @block.tensor
                    def _(t, grp=grp, bank=bank):
                        t.wait_ge(E.ve.h, E.ve.n)
                        for k in range(4):
                            ch = grp * 4 + k
                            inst = t.transpose(bank[:, k * 128:(k + 1) * 128],
                                               nb2[:, ch * 128:(ch + 1) * 128], ident[:])
                        E.pe.inc(inst)

                    @block.vector
                    def _(v, grp=grp, bank=bank):
                        v.wait_ge(E.pe.h, E.pe.n)
                        fence(v, E.ve, v.tensor_copy(tb2[:, grp * 512:(grp + 1) * 512], bank[:]))

                @block.sync
                def _(sy, tt=tt):
                    sy.wait_ge(E.ve.h, E.ve.n)
                    for ch in range(32):
                        sy.dma_start(out=T["xn2T_d"][ch * 128:(ch + 1) * 128, tt * 128:(tt + 1) * 128],
                                     in_=tb2[:, ch * 128:(ch + 1) * 128]).then_inc(E.sy.h, 16)
                        E.sy.n += 16
    E.mark["xn2T_gp"] = 0
    E.mark["xn2T_sy"] = E.sy.n
    E.mark["x1_gp"] = E.gp.n


def _emit_f1(E):
    nc, block, T = E.nc, E.block, E.T
    pb = E.pb
    lora_a, lorab = E.consts["lora_a"], E.consts["lorab"]
    G = 1024
    NGRP = S // G
    NJT = DFFP // 512
    HALF = 16 * 512

    with (nc.sbuf_tensor("f1_xn", [128, 32 * G], BF16) as xnb,
          nc.sbuf_tensor("f1_wg", [128, 3 * HALF], BF16) as wgb,
          nc.sbuf_tensor("f1_wu", [128, 3 * HALF], BF16) as wub,
          nc.sbuf_tensor("f1_u", [128, G], BF16) as ubuf,
          nc.sbuf_tensor("f1_rl", [128, 2 * 512], F32) as rlbuf,
          nc.sbuf_tensor("f1_x3", [128, 4 * 512], BF16) as x3t):

        def wslice(buf, jt, ch, js, jw):
            kh, kc = ch // 16, ch % 16
            slot = (2 * jt + kh) % 3
            base = slot * HALF + (kc // 8) * 8 * jw + (kc % 8) * jw + js * 128
            return buf[:, base:base + 128]

        def emit_wload_half(sy, hidx, pe_gate):
            jt, kh = hidx // 2, hidx % 2
            jw = 512
            j0 = jt * 512
            slot = hidx % 3
            if pe_gate:
                sy.wait_ge(E.pe.h, pe_gate)
            for half in range(2):
                r0 = kh * 2048 + half * 1024
                for wf, buf in ((T["wg_f"], wgb), (T["wu_f"], wub)):
                    src = wf[r0:r0 + 1024, j0:j0 + jw].rearrange("(c p) j -> p c j", p=128)
                    sy.dma_start(out=buf[:, slot * HALF + half * 8 * jw:
                                         slot * HALF + (half * 8 + 8) * jw],
                                 in_=src).then_inc(E.sy.h, 16)
                    E.sy.n += 16
            return E.sy.n

        _scope_barrier(E)
        last_pe_prev_grp = E.pe.n
        ev_hist = []
        st_prev = 0
        for grp in range(NGRP):
            t0 = grp * G

            @block.scalar
            def _(s, t0=t0, gate=last_pe_prev_grp):
                s.wait_ge(E.gp.h, E.mark.get("xn2T_gp", 0))
                if E.mark.get("xn2T_sy", 0):
                    s.wait_ge(E.sy.h, E.mark["xn2T_sy"])
                if gate:
                    s.wait_ge(E.pe.h, gate)
                for ch in range(32):
                    s.dma_start(out=xnb[:, ch * G:(ch + 1) * G],
                                in_=T["xn2T_d"][ch * 128:(ch + 1) * 128, t0:t0 + G]).then_inc(E.sc.h, 16)
                    E.sc.n += 16
            xn_ready = E.sc.n

            for ts in range(G // 512):
                tc0 = ts * 512

                @block.tensor
                def _(t, tc0=tc0, xn_ready=xn_ready, ts=ts, grp=grp, gate=last_pe_prev_grp):
                    t.wait_ge(E.sc.h, xn_ready)
                    t.wait_ge(E.gp.h, max(E.mark["lora_gp"], E.mark.get("lorab_gp", 0)))
                    if ts > 0 or grp > 0:
                        t.wait_ge(E.ve.h, E.ve.n)
                    for ch in range(32):
                        rx = xnb[:, ch * G + tc0: ch * G + tc0 + 512]
                        t.matmul(pb[4][0:16, :], lora_a[:, ch * R:(ch + 1) * R], rx,
                                 start=(ch == 0), stop=(ch == 31))
                    for ch in range(32):
                        rx = xnb[:, ch * G + tc0: ch * G + tc0 + 512]
                        inst = t.matmul(pb[5][0:16, :], lora_a[:, (32 + ch) * R:(33 + ch) * R], rx,
                                        start=(ch == 0), stop=(ch == 31))
                    E.pe.inc(inst)

                @block.vector
                def _(v, tc0=tc0):
                    v.wait_ge(E.pe.h, E.pe.n)
                    v.tensor_copy(ubuf[0:16, tc0:tc0 + 512], pb[4][0:16, :])
                    fence(v, E.ve, v.tensor_copy(ubuf[32:48, tc0:tc0 + 512], pb[5][0:16, :]))
            u_ready = E.ve.n

            half_done = {}

            @block.sync
            def _(sy, gate=last_pe_prev_grp):
                sy.wait_ge(E.cc.h, E.mark.get('wbarrier', E.cc.n))
                for h in (0, 1, 2):
                    emit_wload_half(sy, h, gate)
                    half_done[h] = E.sy.n
            next_half = 3

            it = 0
            sg = 0
            for jt in range(NJT):
                j0 = jt * 512
                jw = 512
                njs = 4
                for ts in range(G // 512):
                    tc0 = ts * 512
                    for js in range(njs):
                        jj = j0 + js * 128
                        par = it % 2
                        pyA, pyB = pb[2 * par], pb[2 * par + 1]
                        ev_gate = ev_hist[-2] if len(ev_hist) >= 2 else 0

                        @block.tensor
                        def _(t, jt=jt, jw=jw, js=js, tc0=tc0, jj=jj,
                              pyA=pyA, pyB=pyB, ev_gate=ev_gate,
                              xn_ready=xn_ready, u_ready=u_ready,
                              wr=None):
                            t.wait_ge(E.sc.h, xn_ready)
                            t.wait_ge(E.sy.h, half_done[2 * jt + 1])
                            t.wait_ge(E.ve.h, max(ev_gate, u_ready))
                            for mat, py, buf in ((0, pyA, wgb), (1, pyB, wub)):
                                for ch in range(32):
                                    lw = wslice(buf, jt, ch, js, jw)
                                    rx = xnb[:, ch * G + tc0: ch * G + tc0 + 512]
                                    t.matmul(py[:], lw, rx, start=(ch == 0), stop=False)
                                lb = lorab[mat * 32:mat * 32 + 16, jj:jj + 128]
                                uu = ubuf[mat * 32:mat * 32 + 16, tc0:tc0 + 512]
                                inst = t.matmul(py[:], lb, uu, start=False, stop=True)
                            E.pe.inc(inst)
                        pe_now = E.pe.n

                        @block.scalar
                        def _(s, pyA=pyA, pe_now=pe_now, par=par,
                              gve=(ev_hist[-2] if len(ev_hist) >= 2 else 0)):
                            s.wait_ge(E.pe.h, pe_now)
                            if gve:
                                s.wait_ge(E.ve.h, gve)
                            inst = s.activation(rlbuf[:, par * 512:(par + 1) * 512], pyA[:], AF.Relu)
                            E.ac.inc(inst)

                        @block.vector
                        def _(v, pyB=pyB, js=js, par=par, st_prev=st_prev):
                            v.wait_ge(E.ac.h, E.ac.n)
                            if st_prev:
                                v.wait_ge(E.gp.h, st_prev)
                            inst = v.tensor_tensor(x3t[:, js * 512:(js + 1) * 512],
                                                   rlbuf[:, par * 512:(par + 1) * 512],
                                                   pyB[:], ALU.mult)
                            fence(v, E.ve, inst)
                        ev_hist.append(E.ve.n)
                        it += 1

                    @block.gpsimd
                    def _(g, j0=j0, jw=jw, njs=njs, t0=t0, tc0=tc0):
                        g.wait_ge(E.ve.h, E.ve.n)
                        src = x3t[:, 0:njs * 512].rearrange("p (c j) -> p c j", j=512)
                        g.dma_start(out=T["x3T_d"][j0:j0 + jw, t0 + tc0:t0 + tc0 + 512]
                                    .rearrange("(c p) j -> p c j", p=128),
                                    in_=src).then_inc(E.gp.h, 16)
                        E.gp.n += 16
                    st_prev = E.gp.n
                    sg += 1

                pe_after = E.pe.n
                loads = []
                while next_half < 2 * NJT and next_half <= 2 * jt + 4:
                    loads.append(next_half)
                    next_half += 1
                if loads:
                    @block.sync
                    def _(sy, loads=tuple(loads), gate=pe_after):
                        for h in loads:
                            emit_wload_half(sy, h, gate)
                            half_done[h] = E.sy.n
            last_pe_prev_grp = E.pe.n
    E.mark["x3T_gp"] = E.gp.n


def _emit_f2(E):
    nc, block, T = E.nc, E.block, E.T
    pb = E.pb
    lora_a, lorab = E.consts["lora_a"], E.consts["lorab"]
    NTG = 4
    NCH = DFF // 128
    RANK_NCH = [r // 128 for r in D_SHARD_ROWS]
    RANK_C0 = [sum(RANK_NCH[:r]) for r in range(8)]
    SLOTW = 11 * 512

    with (nc.sbuf_tensor("f2_x3", [128, NCH * 512], BF16) as x3b,
          nc.sbuf_tensor("f2_wd", [128, 3 * SLOTW], BF16) as wdb,
          nc.sbuf_tensor("f2_ud", [128, 512], BF16) as udb,
          nc.sbuf_tensor("f2_y3", [128, 4 * 512], F32) as y3t):

        _scope_barrier(E)
        for tg in range(NTG):
            t0 = tg * 512
            gate_pe = E.pe.n

            @block.scalar
            def _(s, t0=t0, gate_pe=gate_pe):
                s.wait_ge(E.gp.h, E.mark["x3T_gp"])
                if gate_pe:
                    s.wait_ge(E.pe.h, gate_pe)
                for ch in range(NCH):
                    s.dma_start(out=x3b[:, ch * 512:(ch + 1) * 512],
                                in_=T["x3T_d"][ch * 128:(ch + 1) * 128, t0:t0 + 512]).then_inc(E.sc.h, 16)
                    E.sc.n += 16
            x3_ready = E.sc.n

            wd_ready = {}
            pe_rank = {}

            def emit_rank_load(sy, m, r, gate):
                if gate:
                    sy.wait_ge(E.pe.h, gate)
                ri = m * 8 + r
                slot = ri % 3
                g0 = r * D_PAD
                nrow = D_SHARD_ROWS[r]
                src = T["wd_f"][g0:g0 + nrow, m * 512:m * 512 + 512]\
                    .rearrange("(c p) j -> p c j", p=128)
                sy.dma_start(out=wdb[:, slot * SLOTW:slot * SLOTW + (nrow // 128) * 512],
                             in_=src).then_inc(E.sy.h, 16)
                E.sy.n += 16
                wd_ready[ri] = E.sy.n

            @block.sync
            def _(sy, tg=tg, gate_pe=gate_pe):
                sy.wait_ge(E.cc.h, E.mark.get('wbarrier', E.cc.n))
                for ri in range(3):
                    emit_rank_load(sy, ri // 8, ri % 8, gate_pe)

            ud_gate = 0
            ev_prev_m = (E.ve.n, E.ac.n)
            for m in range(8):
                m0 = m * 512
                for r in range(8):
                    ri = m * 8 + r
                    slot = ri % 3
                    is_first = (ri == 0)
                    is_last_rank = (r == 7)

                    @block.tensor
                    def _(t, m=m, r=r, ri=ri, slot=slot, x3_ready=x3_ready,
                          ev_prev_m=ev_prev_m):
                        t.wait_ge(E.sc.h, x3_ready)
                        t.wait_ge(E.sy.h, wd_ready[ri])
                        if r == 0:
                            if ev_prev_m[0]:
                                t.wait_ge(E.ve.h, ev_prev_m[0])
                            if ev_prev_m[1]:
                                t.wait_ge(E.ac.h, ev_prev_m[1])
                        nch_r = RANK_NCH[r]
                        for k in range(nch_r):
                            ch = RANK_C0[r] + k
                            rw = wdb[:, slot * SLOTW + k * 512:slot * SLOTW + (k + 1) * 512]
                            for tsub in range(4):
                                lx = x3b[:, ch * 512 + tsub * 128:ch * 512 + tsub * 128 + 128]
                                inst = t.matmul(pb[tsub][:], lx, rw,
                                                start=(ch == 0), stop=False)
                            if m == 0:
                                inst = t.matmul(pb[4][0:16, :],
                                                lora_a[:, (64 + ch) * R:(65 + ch) * R],
                                                x3b[:, ch * 512:(ch + 1) * 512],
                                                start=(ch == 0), stop=(ch == NCH - 1))
                        E.pe.inc(inst)
                    pe_rank[ri] = E.pe.n

                    if ri + 3 < 64:
                        @block.sync
                        def _(sy, m2=(ri + 3) // 8, r2=(ri + 3) % 8, gate=pe_rank[ri]):
                            emit_rank_load(sy, m2, r2, gate)

                if m == 0:
                    @block.vector
                    def _(v):
                        v.wait_ge(E.pe.h, E.pe.n)
                        fence(v, E.ve, v.tensor_copy(udb[64:80, :], pb[4][0:16, :]))
                    ud_gate = E.ve.n

                @block.tensor
                def _(t, m0=m0, ud_gate=ud_gate):
                    t.wait_ge(E.ve.h, ud_gate)
                    for tsub in range(4):
                        inst = t.matmul(pb[tsub][:], udb[64:80, tsub * 128:(tsub + 1) * 128],
                                        lorab[64:80, m0:m0 + 512], start=False, stop=True)
                    E.pe.inc(inst)
                pe_m = E.pe.n

                @block.scalar
                def _(s, pe_m=pe_m):
                    s.wait_ge(E.pe.h, pe_m)
                    s.wait_ge(E.gp.h, E.gp.n)
                    for tsub in range(4):
                        inst = s.activation(y3t[:, tsub * 512:(tsub + 1) * 512],
                                            pb[tsub][:], AF.Copy)
                    E.ac.inc(inst)
                @block.gpsimd
                def _(g, t0=t0, m0=m0):
                    g.wait_ge(E.ac.h, E.ac.n)
                    src = y3t[:].rearrange("p (c j) -> p c j", j=512)
                    g.dma_start(out=T["y3_d"][t0:t0 + 512, m0:m0 + 512]
                                .rearrange("(c p) j -> p c j", p=128),
                                in_=src).then_inc(E.gp.h, 16)
                    E.gp.n += 16
                ev_prev_m = (0, E.ac.n)
            E.mark[f"y3_tg{tg}_gp"] = E.gp.n
    E.mark["y3_gp"] = E.gp.n


def _emit_f3(E):
    nc, block, T = E.nc, E.block, E.T
    with (nc.sbuf_tensor("f3_y", [128, DM], F32) as yb,
          nc.sbuf_tensor("f3_x1", [128, DM], F32) as x1b,
          nc.sbuf_tensor("f3_scr", [128, DM], F32) as scr,
          nc.sbuf_tensor("f3_st", [128, 4], F32) as stat,
          nc.sbuf_tensor("f3_o", [128, DM], F32) as ob):
        for tt in range(NT):
            tg = tt // 4

            @block.sync
            def _(sy, tt=tt, tg=tg):
                sy.wait_ge(E.gp.h, max(E.mark[f"y3_tg{tg}_gp"], E.mark.get("x1_gp", 0)))
                sy.wait_ge(E.ve.h, E.ve.n)
                sy.dma_start(out=yb[:], in_=T["y3_d"][tt * 128:(tt + 1) * 128, :]).then_inc(E.sy.h, 16)
                sy.dma_start(out=x1b[:], in_=T["x1_d"][tt * 128:(tt + 1) * 128, :]).then_inc(E.sy.h, 16)
                E.sy.n += 32

            @block.vector
            def _(v):
                v.wait_ge(E.sy.h, E.sy.n)
            _ln_core(E, yb, scr, stat, DM)

            @block.vector
            def _(v):
                v.wait_ge(E.ac.h, E.ac.n)
                v.wait_ge(E.st.h, E.st.n)
                fence(v, E.ve, v.tensor_scalar(yb[:], yb[:], stat[:, 3:4], None, ALU.mult))
                fence(v, E.ve, v.tensor_tensor(ob[:], yb[:], x1b[:], ALU.add))

            @block.gpsimd
            def _(g, tt=tt):
                g.wait_ge(E.ve.h, E.ve.n)
                g.dma_start(out=T["out_t"][tt * 128:(tt + 1) * 128, :], in_=ob[:]).then_inc(E.st.h, 16)
                E.st.n += 16


def kernel(**inputs):
    maps = _prep(inputs)
    nc = build()
    res = run_bass_kernel_spmd(nc, maps, core_ids=list(range(8)))
    return np.stack([res.results[i]["out"] for i in range(8)], axis=0)
